# version 08
# baseline (speedup 1.0000x reference)
import numpy as np
from contextlib import ExitStack

import concourse.bass as bass
import concourse.mybir as mybir
from concourse.bass_utils import run_bass_kernel_spmd

F32 = mybir.dt.float32
BF16 = mybir.dt.bfloat16
AF = mybir.ActivationFunctionType
ALU = mybir.AluOpType
AX = mybir.AxisListType

C = 1024
S = 4096
NH = 16
CK = C // 128
NCORE = 8
HID = 4096
HK = HID // 128

ENGS = ("tensor", "vector", "scalar", "gpsimd", "sync")

SHARED = {}



class Sem:
    __slots__ = ("h", "n")

    def __init__(self, h):
        self.h = h
        self.n = 0


class Prog:

    def __init__(self, nc, stack):
        self.nc = nc
        self.stack = stack
        self.q = {e: [] for e in ENGS}
        self._sid = 0
        self.sems = []
        self.dma_sems = {}

    _gsid = [0]

    def sem(self, name=None):
        Prog._gsid[0] += 1
        h = self.nc.alloc_semaphore(
            name=f"{name or 'sm'}_{Prog._gsid[0]}")
        s = Sem(h)
        self.sems.append(s)
        return s

    def op(self, eng, fn, inc=None):
        incs = [] if inc is None else ([inc] if isinstance(inc, tuple)
                                       else list(inc))
        for sem, k in incs:
            sem.n += k
        if incs:
            h0, k0 = incs[0][0].h, incs[0][1]
            self.q[eng].append(
                lambda e, fn=fn, h0=h0, k0=k0: fn(e).then_inc(h0, k0))
            if len(incs) > 1:
                v0 = incs[0][0].n
                self.q[eng].append(
                    lambda e, h0=h0, v0=v0: e.wait_ge(h0, v0))
                for sem, k in incs[1:]:
                    self.q[eng].append(
                        lambda e, h=sem.h, k=k: e.nop().then_inc(h, k))
        else:
            self.q[eng].append(fn)

    def sig(self, eng, inc):
        self.op(eng, lambda e: e.nop(), inc=inc)

    def wait(self, eng, sem, val=None):
        v = sem.n if val is None else val
        if v > 0:
            self.q[eng].append(lambda e, s=sem.h, v=v: e.wait_ge(s, v))

    def dma(self, eng, out, in_, sem, **kw):
        self.dma_sems[sem] = eng
        self.op(eng, lambda e, o=out, i=in_, kw=kw: e.dma_start(o, i, **kw),
                inc=(sem, 16))

    def drain_dmas(self):
        for sem, eng in self.dma_sems.items():
            self.wait(eng, sem)

    def flush(self, name=None):
        qs = {e: q for e, q in self.q.items() if q}
        self.q = {e: [] for e in ENGS}
        if not qs:
            return
        with self.nc.Block(name) as b:
            for e, q in qs.items():
                def player(eng, q=q):
                    for fn in q:
                        fn(eng)
                getattr(b, e)(player)

    def psem(self, name=None):
        Prog._gsid[0] += 1
        h = self.nc.alloc_semaphore(name=f"{name or 'ps'}_{Prog._gsid[0]}")
        return Sem(h)

    def end_phase(self, name):
        self.drain_dmas()
        self.flush(name + "_fin")
        self.nc.clear_and_free_semaphores([s.h for s in self.sems])
        self.sems = []
        self.dma_sems = {}


class Guard:

    def __init__(self, prog, name):
        self.p = prog
        self.wr = prog.sem(name + "w")
        self.rd = prog.sem(name + "r")
        self.written = False

    def acquire(self, eng):
        if self.written:
            self.p.wait(eng, self.rd)
        self.written = True

    def wr_inc(self):
        return (self.wr, 1)

    def rd_inc(self):
        return (self.rd, 1)


class Bank(Guard):
    def __init__(self, prog, name, width=512, dtype=F32, part=128, stack=None):
        super().__init__(prog, name)
        st = stack or prog.stack
        self.t = st.enter_context(
            prog.nc.psum_tensor(name, [part, width], dtype))


def sb(prog, name, shape, dtype, stack=None):
    st = stack or prog.stack
    return st.enter_context(prog.nc.sbuf_tensor(name, shape, dtype))


def full(t):
    n = len(t.shape)
    if n == 1:
        return t[:]
    if n == 2:
        return t[:, :]
    if n == 3:
        return t[:, :, :]
    return t[:, :, :, :]


class LBuf:

    def __init__(self, prog, name, shape, dtype, eng="gpsimd", stack=None):
        self.p = prog
        self.t = sb(prog, name, shape, dtype, stack)
        self.rd = prog.sem(name + "r")
        self.ld = prog.sem(name + "l")
        self.eng = eng
        self.rounds = 0
        self.ldv = 0

    def load(self, in_, dst=None, **kw):
        if self.rounds > 0:
            self.p.wait(self.eng, self.rd)
        self.p.dma(self.eng, full(self.t) if dst is None else dst, in_,
                   self.ld, **kw)
        self.ldv = self.ld.n
        self.rounds += 1

    def rwait(self, eng):
        self.p.wait(eng, self.ld, self.ldv)

    def rd_inc(self):
        return (self.rd, 1)


class OBuf:

    def __init__(self, prog, name, shape, dtype, eng="sync", stack=None):
        self.p = prog
        self.t = sb(prog, name, shape, dtype, stack)
        self.wr = prog.sem(name + "w")
        self.st = prog.sem(name + "s")
        self.eng = eng
        self.rounds = 0

    def awrite(self, eng):
        if self.rounds > 0:
            self.p.wait(eng, self.st, 16 * self.rounds)

    def wr_inc(self):
        return (self.wr, 1)

    def store(self, out_ap, src=None):
        self.p.wait(self.eng, self.wr)
        self.p.dma(self.eng, out_ap, full(self.t) if src is None else src,
                   self.st)
        self.rounds += 1


def rearr(handle, t0=None, t1=None):
    ap = handle.ap().rearrange("(a p) t -> p a t", p=128)
    if t0 is not None:
        ap = ap[:, :, t0:t1]
    return ap


def bcast_ap(handle, offset, n, parts=128):
    return bass.AP(handle, offset, [[0, parts], [1, n]])


def make_ident(prog, tile, n=128):
    prog.op("gpsimd", lambda g: g.memset(tile[:, :], 0.0))
    prog.op("gpsimd", lambda g: g.affine_select(
        out=tile[:, :], in_=tile[:, :], compare_op=ALU.not_equal, fill=1.0,
        base=0, pattern=[[-1, n]], channel_multiplier=1))


def make_e0(prog, name, stack=None):
    e0 = sb(prog, name, [128, 128], F32, stack)
    prog.op("gpsimd", lambda g: g.memset(e0[:, :], 0.0))
    prog.op("gpsimd", lambda g: g.memset(e0[0:1, :], 1.0))
    return e0


def bias_cols_load(prog, dst, src_handle, n_chunks, offset=0, sem=None):
    src = bass.AP(src_handle, offset, [[1, 128], [128, n_chunks]])
    s = sem or prog.sem()
    prog.dma("gpsimd", dst[:, 0:n_chunks], src, s,
             allow_slow_non_contiguous=True)
    return s


def roundtrip_bcast(prog, scr, row_src_ap, lbuf, bc_src_ap, sem, prior=()):
    for lb in prior:
        prog.wait("gpsimd", lb.ld)
    prog.dma("gpsimd", scr.ap(), row_src_ap, sem)
    prog.wait("gpsimd", sem)
    lbuf.load(bc_src_ap)



def phase_T(prog, io):
    x_in, xfm = io["x_in"], io["xfm"]
    idf = sb(prog, "t_idf", [128, 128], F32)
    xt = [LBuf(prog, f"t_xt{i}", [128, CK, 128], F32, "sync")
          for i in range(2)]
    ot = [OBuf(prog, f"t_ot{i}", [128, CK, 128], F32) for i in range(2)]
    banks = [Bank(prog, f"t_pb{i}", 512) for i in range(2)]

    make_ident(prog, idf)
    prog.flush("t_init")

    for tc in range(S // 128):
        bi = tc % 2
        xt[bi].load(x_in.ap()[tc * 128:(tc + 1) * 128, :]
                    .rearrange("p (a c) -> p a c", c=128))
        ot[bi].awrite("vector")
        for ci in range(CK):
            bk = banks[ci % 2]
            bk.acquire("tensor")
            if ci == 0:
                xt[bi].rwait("tensor")
            prog.op("tensor", lambda t, b=bk, s=xt[bi].t[:, ci, :]:
                    t.matmul(b.t[:, 0:128], s, idf[:, :], is_transpose=True),
                    inc=bk.wr_inc())
            prog.wait("vector", bk.wr)
            incs = [bk.rd_inc()]
            if ci == CK - 1:
                incs += [xt[bi].rd_inc(), ot[bi].wr_inc()]
            prog.op("vector", lambda v, b=bk, d=ot[bi].t[:, ci, :]:
                    v.tensor_copy(d, b.t[:, 0:128]), inc=incs)
        ot[bi].store(rearr(xfm, tc * 128, (tc + 1) * 128))



def phase_OUTT(prog, io):
    x4, out_h = io["x4"], io["out"]
    idf = sb(prog, "ou_idf", [128, 128], F32)
    xt = [LBuf(prog, f"ou_xt{i}", [128, CK, 128], F32, "sync")
          for i in range(2)]
    ot = [OBuf(prog, f"ou_ot{i}", [128, CK, 128], F32) for i in range(2)]
    banks = [Bank(prog, f"ou_pb{i}", 512) for i in range(2)]

    make_ident(prog, idf)
    prog.flush("ou_init")

    for tc in range(S // 128):
        bi = tc % 2
        xt[bi].load(rearr(x4, tc * 128, (tc + 1) * 128))
        ot[bi].awrite("vector")
        for ci in range(CK):
            bk = banks[ci % 2]
            bk.acquire("tensor")
            if ci == 0:
                xt[bi].rwait("tensor")
            prog.op("tensor", lambda t, b=bk, s=xt[bi].t[:, ci, :]:
                    t.matmul(b.t[:, 0:128], s, idf[:, :], is_transpose=True),
                    inc=bk.wr_inc())
            prog.wait("vector", bk.wr)
            incs = [bk.rd_inc()]
            if ci == CK - 1:
                incs += [xt[bi].rd_inc(), ot[bi].wr_inc()]
            prog.op("vector", lambda v, b=bk, d=ot[bi].t[:, ci, :]:
                    v.tensor_copy(d, b.t[:, 0:128]), inc=incs)
        ot[bi].store(out_h.ap()[tc * 128:(tc + 1) * 128, :]
                     .rearrange("p (a c) -> p a c", c=128))



def phase_LN(prog, io, a_name, b_name, o_name, tag):
    a_h, b_h, o_h = io[a_name], io[b_name], io[o_name]

    TS = 512
    af = [LBuf(prog, f"{tag}_a{i}", [128, CK, TS], F32, "sync")
          for i in range(2)]
    bf = [LBuf(prog, f"{tag}_b{i}", [128, CK, TS], F32, "sync")
          for i in range(2)]
    pre = [sb(prog, f"{tag}_p{i}", [128, CK, TS], F32) for i in range(2)]
    preg = [Guard(prog, f"{tag}_pg{i}") for i in range(2)]
    preb = sb(prog, f"{tag}_pb", [128, CK, TS], BF16)
    sqb = sb(prog, f"{tag}_sq", [128, CK, TS], BF16)
    pbg = Guard(prog, f"{tag}_pbg")
    sqg = Guard(prog, f"{tag}_sqg")
    ones = sb(prog, f"{tag}_ones", [128, 128], BF16)
    stat = sb(prog, f"{tag}_st", [128, 3, TS], F32)
    stg = Guard(prog, f"{tag}_stg")
    epsc = sb(prog, f"{tag}_eps", [128, 1], F32)
    ob = [OBuf(prog, f"{tag}_o{i}", [128, CK, TS], F32) for i in range(2)]
    bS = Bank(prog, f"{tag}_bS")
    bQ = Bank(prog, f"{tag}_bQ")

    prog.op("gpsimd", lambda g: g.memset(ones[:, :], 1.0))
    prog.op("gpsimd", lambda g: g.memset(epsc[:, :], 1e-5))
    prog.flush(f"{tag}_init")

    for s in range(S // TS):
        bi = s % 2
        t0, t1 = s * TS, (s + 1) * TS
        af[bi].load(rearr(a_h, t0, t1))
        bf[bi].load(rearr(b_h, t0, t1))
        preg[bi].acquire("vector")
        af[bi].rwait("vector")
        bf[bi].rwait("vector")
        for ci in range(CK):
            incs = ([preg[bi].wr_inc(), af[bi].rd_inc(), bf[bi].rd_inc()]
                    if ci == CK - 1 else None)
            prog.op("vector", lambda v, ci=ci, bi=bi: v.tensor_tensor(
                pre[bi][:, ci, :], af[bi].t[:, ci, :], bf[bi].t[:, ci, :],
                op=ALU.add), inc=incs)
        pbg.acquire("scalar")
        sqg.acquire("scalar")
        prog.wait("scalar", preg[bi].wr)
        for ci in range(CK):
            prog.op("scalar", lambda sc, ci=ci, bi=bi: sc.activation(
                preb[:, ci, :], pre[bi][:, ci, :], AF.Identity),
                inc=pbg.wr_inc() if ci == CK - 1 else None)
        for ci in range(CK):
            incs = [sqg.wr_inc()] if ci == CK - 1 else None
            if ci == CK - 1:
                incs.append(preg[bi].rd_inc())
            prog.op("scalar", lambda sc, ci=ci, bi=bi: sc.activation(
                sqb[:, ci, :], pre[bi][:, ci, :], AF.Square), inc=incs)
        bS.acquire("tensor")
        prog.wait("tensor", pbg.wr)
        for ci in range(CK):
            incs = [bS.wr_inc(), pbg.rd_inc()] if ci == CK - 1 else None
            prog.op("tensor", lambda t, ci=ci: t.matmul(
                bS.t[:, :], ones[:, :], preb[:, ci, :],
                start=(ci == 0), stop=(ci == CK - 1)), inc=incs)
        bQ.acquire("tensor")
        prog.wait("tensor", sqg.wr)
        for ci in range(CK):
            incs = [bQ.wr_inc(), sqg.rd_inc()] if ci == CK - 1 else None
            prog.op("tensor", lambda t, ci=ci: t.matmul(
                bQ.t[:, :], ones[:, :], sqb[:, ci, :],
                start=(ci == 0), stop=(ci == CK - 1)), inc=incs)
        stg.acquire("vector")
        prog.wait("vector", bS.wr)
        prog.wait("vector", bQ.wr)
        prog.op("vector", lambda v: v.tensor_scalar(
            stat[:, 0, :], bS.t[:, :], 1.0 / C, None, op0=ALU.mult),
            inc=bS.rd_inc())
        prog.op("vector", lambda v: v.tensor_scalar(
            stat[:, 1, :], bQ.t[:, :], 1.0 / C, None, op0=ALU.mult),
            inc=bQ.rd_inc())
        prog.op("vector", lambda v: v.tensor_tensor(
            stat[:, 2, :], stat[:, 0, :], stat[:, 0, :], op=ALU.mult))
        prog.op("vector", lambda v: v.tensor_tensor(
            stat[:, 1, :], stat[:, 1, :], stat[:, 2, :],
            op=ALU.subtract), inc=stg.wr_inc())
        prog.wait("scalar", stg.wr)
        prog.op("scalar", lambda sc: sc.activation(
            stat[:, 1, :], stat[:, 1, :], AF.Ln, bias=epsc[:, 0:1]))
        prog.op("scalar", lambda sc: sc.activation(
            stat[:, 2, :], stat[:, 1, :], AF.Exp, scale=-0.5),
            inc=stg.wr_inc())
        ob[bi].awrite("vector")
        prog.wait("vector", stg.wr)
        for ci in range(CK):
            prog.op("vector", lambda v, ci=ci, bi=bi: v.tensor_tensor(
                ob[bi].t[:, ci, :], pre[bi][:, ci, :], stat[:, 0, :],
                op=ALU.subtract))
            incs = None
            if ci == CK - 1:
                incs = [preg[bi].rd_inc(), stg.rd_inc(), ob[bi].wr_inc()]
            prog.op("vector", lambda v, ci=ci, bi=bi: v.tensor_tensor(
                ob[bi].t[:, ci, :], ob[bi].t[:, ci, :], stat[:, 2, :],
                op=ALU.mult), inc=incs)
        ob[bi].store(rearr(o_h, t0, t1))



def phase_L0F(prog, io):
    nc = prog.nc
    x1, w_h, al_h, bi_h, out_h = (io["x1"], io["L0_ffn_weight"],
                                  io["L0_ffn_alpha"], io["L0_ffn_bias"],
                                  io["ffnout"])
    drow_scr = nc.dram_tensor("l0f_drow", [2], F32, kind="Internal")

    wf = LBuf(prog, "l0f_wf", [128, CK, C], F32, "gpsimd")
    colsum = sb(prog, "l0f_cs", [128, CK], F32)
    csg = Guard(prog, "l0f_csg")
    tern = sb(prog, "l0f_tern", [128, CK, C], BF16)
    teg = Guard(prog, "l0f_teg")
    g01 = sb(prog, "l0f_g01", [128, C], BF16)
    sg = sb(prog, "l0f_sg", [128, C], BF16)
    sgg = Guard(prog, "l0f_sgg")
    wT = sb(prog, "l0f_wT", [128, CK, C], BF16)
    wTg = Guard(prog, "l0f_wTg")
    onesf = sb(prog, "l0f_onef", [128, 2], F32)
    row = sb(prog, "l0f_row", [1, 2], F32)
    rwg = Guard(prog, "l0f_rwg")
    dcol = LBuf(prog, "l0f_dcol", [128, 1], F32, "gpsimd")
    acol = LBuf(prog, "l0f_acol", [128, 1], F32, "gpsimd")
    bcol = sb(prog, "l0f_bcol", [128, CK], F32)
    idb = sb(prog, "l0f_idb", [128, 128], BF16)
    bS = Bank(prog, "l0f_bS")
    tb = [Bank(prog, f"l0f_tb{i}", 1024, BF16) for i in range(2)]
    mb = [Bank(prog, f"l0f_mb{i}") for i in range(2)]
    rs = prog.sem()

    idf32 = sb(prog, "l0f_idf32", [128, 128], F32)
    make_ident(prog, idf32)
    prog.flush("l0f_id0")
    prog.op("vector", lambda v: v.tensor_copy(idb[:, :], idf32[:, :]))
    prog.op("gpsimd", lambda g: g.memset(onesf[:, :], 1.0))
    wf.load(rearr(w_h))
    acol.load(bcast_ap(al_h, 0, 1))
    bias_cols_load(prog, bcol, bi_h, CK)
    prog.flush("l0f_init")

    csg.acquire("vector")
    wf.rwait("vector")
    for a in range(CK):
        prog.op("vector", lambda v, a=a: v.tensor_reduce(
            colsum[:, a:a + 1], wf.t[:, a, :], axis=AX.X, op=ALU.add,
            apply_absolute_value=True),
            inc=csg.wr_inc() if a == CK - 1 else None)
    bS.acquire("tensor")
    prog.wait("tensor", csg.wr)
    prog.op("tensor", lambda t: t.matmul(
        bS.t[0:2, 0:CK], onesf[:, 0:2], colsum[:, 0:CK],
        start=True, stop=True), inc=[bS.wr_inc(), csg.rd_inc()])
    rwg.acquire("vector")
    prog.wait("vector", bS.wr)
    csrow = sb(prog, "l0f_csrow", [2, CK], F32)
    trow = sb(prog, "l0f_trow", [1, CK], F32)
    prog.op("vector", lambda v: v.tensor_copy(
        csrow[0:2, 0:CK], bS.t[0:2, 0:CK]), inc=bS.rd_inc())
    prog.wait("scalar", bS.rd)
    prog.op("scalar", lambda sc: sc.activation(
        trow[0:1, 0:CK], csrow[0:1, 0:CK], AF.Identity,
        accum_out=row[0:1, 0:1]))
    prog.flush("l0f_delta0")
    prog.op("vector", lambda v: v.tensor_scalar(
        row[0:1, 1:2], row[0:1, 0:1], 0.7 / (C * C), None, op0=ALU.mult),
        inc=rwg.wr_inc())
    prog.wait("gpsimd", rwg.wr)
    prog.dma("gpsimd", drow_scr.ap()[0:1], row[0:1, 1:2], rs)
    prog.sig("gpsimd", rwg.rd_inc())
    prog.wait("gpsimd", rs)
    dcol.load(bcast_ap(drow_scr, 0, 1))
    teg.acquire("vector")
    dcol.rwait("vector")
    ab = sb(prog, "l0f_ab", [128, C], F32)
    for a in range(CK):
        sgg.acquire("scalar")
        prog.op("scalar", lambda sc, a=a: sc.activation(
            ab[:, :], wf.t[:, a, :], AF.Abs))
        prog.op("scalar", lambda sc, a=a: sc.activation(
            sg[:, :], wf.t[:, a, :], AF.Sign), inc=sgg.wr_inc())
        prog.wait("vector", sgg.wr)
        prog.op("vector", lambda v, a=a: v.tensor_scalar(
            g01[:, :], ab[:, :], dcol.t[:, 0:1], None, op0=ALU.is_gt))
        incs = [sgg.rd_inc()]
        if a == CK - 1:
            incs += [dcol.rd_inc(), wf.rd_inc(), teg.wr_inc()]
        prog.op("vector", lambda v, a=a: v.tensor_tensor(
            tern[:, a, :], g01[:, :], sg[:, :], op=ALU.mult), inc=incs)
    prog.flush("l0f_tern")

    if globals().get("DBG_L0F") == 1:
        dsem = prog.sem()
        for a in range(CK):
            prog.dma("gpsimd", rearr(out_h, 0, 1024)[:, a, :],
                     tern[:, a, :], dsem)
        prog.dma("gpsimd", rearr(out_h, 1024, 1025)[:, 1, :],
                 dcol.t[:, 0:1], dsem, allow_slow_non_contiguous=True)
        prog.dma("gpsimd", out_h.ap()[0:2, 1025:1025 + CK], csrow[0:2, 0:CK],
                 dsem)
        prog.dma("gpsimd", out_h.ap()[0:1, 1040:1042], row[0:1, 0:2], dsem)
        prog.drain_dmas()
        prog.flush("l0f_dbg")
        return

    wTg.acquire("vector")
    prog.wait("tensor", teg.wr)
    for a in range(CK):
        for icc in range(CK):
            bk = tb[(a * CK + icc) % 2]
            bk.acquire("tensor")
            prog.op("tensor", lambda t, b=bk, a=a, icc=icc: t.matmul(
                b.t[:, 0:128], tern[:, a, icc * 128:(icc + 1) * 128],
                idb[:, :], is_transpose=True), inc=bk.wr_inc())
            prog.wait("vector", bk.wr)
            last = (a == CK - 1 and icc == CK - 1)
            incs = [bk.rd_inc()] + ([wTg.wr_inc(), teg.rd_inc()] if last
                                    else [])
            prog.op("vector", lambda v, b=bk, a=a, icc=icc: v.tensor_copy(
                wT[:, icc, a * 128:(a + 1) * 128], b.t[:, 0:128]), inc=incs)
    prog.flush("l0f_transp")

    if globals().get("DBG_L0F") == 2:
        dsem = prog.sem()
        for ci in range(CK):
            prog.dma("gpsimd", rearr(out_h, 0, 1024)[:, ci, :],
                     wT[:, ci, :], dsem)
        prog.drain_dmas()
        prog.flush("l0f_dbg2")
        return

    xs = [LBuf(prog, f"l0f_x{i}", [128, CK, 512], BF16, "gpsimd")
          for i in range(2)]
    ob = [OBuf(prog, f"l0f_ob{i}", [128, CK, 512], F32) for i in range(2)]
    prog.wait("tensor", wTg.wr)
    acol.rwait("scalar")
    for s in range(S // 512):
        bi = s % 2
        t0, t1 = s * 512, (s + 1) * 512
        xs[bi].load(rearr(x1, t0, t1))
        xs[bi].rwait("tensor")
        ob[bi].awrite("scalar")
        for oc in range(CK):
            bk = mb[oc % 2]
            bk.acquire("tensor")
            for ci in range(CK):
                incs = None
                if ci == CK - 1:
                    incs = [bk.wr_inc()]
                    if oc == CK - 1:
                        incs.append(xs[bi].rd_inc())
                prog.op("tensor", lambda t, b=bk, oc=oc, ci=ci, bi=bi:
                        t.matmul(b.t[:, :],
                                 wT[:, ci, oc * 128:(oc + 1) * 128],
                                 xs[bi].t[:, ci, :],
                                 start=(ci == 0), stop=(ci == CK - 1)),
                        inc=incs)
            prog.wait("scalar", bk.wr)
            incs = [bk.rd_inc()] + ([ob[bi].wr_inc()] if oc == CK - 1 else [])
            prog.op("scalar", lambda sc, b=bk, oc=oc, bi=bi: sc.activation(
                ob[bi].t[:, oc, :], b.t[:, :], AF.Identity,
                bias=bcol[:, oc:oc + 1], scale=acol.t[:, 0:1]), inc=incs)
        ob[bi].store(rearr(out_h, t0, t1))



def phase_L0A(prog, io):
    nc = prog.nc
    xfm = io["xfm"]
    wqkv_h, bqkv_h = io["L0_attn_w_qkv"], io["L0_attn_b_qkv"]
    wout_h, bout_h = io["L0_attn_w_out"], io["L0_attn_b_out"]
    out_h = io["attnout"]

    TS = 512
    NS = S // TS
    K_sb = sb(prog, "l0a_K", [128, CK, S], BF16)
    V_sb = sb(prog, "l0a_V", [128, 32, NH * 65], BF16)
    bq_col = sb(prog, "l0a_bq", [128, CK], F32)
    bk_col = sb(prog, "l0a_bk", [128, CK], F32)
    bo_col = sb(prog, "l0a_bo", [128, CK], F32)
    masks = sb(prog, "l0a_msk", [128, 4, TS], BF16)
    ones_b = sb(prog, "l0a_one", [128, 128], BF16)
    mxrow = sb(prog, "l0a_mxr", [128, 512], F32)
    tmpr = sb(prog, "l0a_tmpr", [128, 512], BF16)
    tmg = Guard(prog, "l0a_tmg")
    kmax = sb(prog, "l0a_kmax", [128, 2], F32)
    dnb = sb(prog, "l0a_dnb", [128, 2, 512], BF16)
    e0b = sb(prog, "l0a_e0b", [128, 128], BF16)
    ld = prog.sem("l0a_ld")

    prog.op("gpsimd", lambda g: g.memset(ones_b[:, :], 1.0))
    prog.op("gpsimd", lambda g: g.memset(mxrow[:, :], 0.0))
    prog.op("gpsimd", lambda g: g.memset(masks[:, :, :], 1.0))
    prog.op("gpsimd", lambda g: g.memset(dnb[:, :, :], 0.0))
    prog.op("gpsimd", lambda g: g.memset(e0b[:, :], 0.0))
    prog.op("gpsimd", lambda g: g.memset(e0b[0:1, :], 1.0))
    for c4 in range(4):
        prog.op("gpsimd", lambda g, c4=c4: g.affine_select(
            out=masks[:, c4, :], in_=masks[:, c4, :], compare_op=ALU.is_ge,
            fill=0.0, base=128 * c4, pattern=[[-1, TS]],
            channel_multiplier=1))
    prog.op("gpsimd", lambda g: g.memset(
        bass.AP(V_sb, 64, [[32 * NH * 65, 128], [NH * 65, 32], [65, NH]]),
        1.0))
    bias_cols_load(prog, bq_col, bqkv_h, CK, 0, ld)
    bias_cols_load(prog, bk_col, bqkv_h, CK, C, ld)
    bias_cols_load(prog, bo_col, bout_h, CK, 0, ld)
    if SHARED.get("overlap_prep"):
        csem = prog.psem("l1f_cast")
        SHARED["l1f_csem"] = csem
        SHARED["l1f_cast_idx"] = 0
        for _ in range(2):
            SHARED["l1f_cast_idx"] += emit_l1f_cast(
                prog, io, SHARED["l1f_cast_idx"], csem)
    prog.flush("l0a_init")

    sub = ExitStack()
    wk = LBuf(prog, "l0a_wk", [128, CK, C], BF16, "gpsimd", sub)
    wv = LBuf(prog, "l0a_wv", [128, CK, C], BF16, "gpsimd", sub)
    bv_row = LBuf(prog, "l0a_bv", [128, C], F32, "gpsimd", sub)
    for ci in range(CK):
        prog.dma("gpsimd", wk.t[:, ci, :],
                 wqkv_h.ap()[ci * 128:(ci + 1) * 128, C:2 * C], wk.ld)
        prog.dma("gpsimd", wv.t[:, ci, :],
                 wqkv_h.ap()[ci * 128:(ci + 1) * 128, 2 * C:3 * C], wv.ld)
    wk.ldv, wk.rounds = wk.ld.n, 1
    wv.ldv, wv.rounds = wv.ld.n, 1
    bv_row.load(bcast_ap(bqkv_h, 2 * C, C))
    xs = [LBuf(prog, f"l0a_x{i}", [128, CK, TS], BF16, "gpsimd", sub)
          for i in range(2)]
    sqk = sb(prog, "l0a_sqk", [128, TS], BF16, sub)
    sqkg = Guard(prog, "l0a_sqkg")
    kg = Guard(prog, "l0a_kg")
    pb = [Bank(prog, f"l0a_pp{i}", stack=sub) for i in range(4)]
    nb = [Bank(prog, f"l0a_nb{i}", stack=sub) for i in range(2)]

    wk.rwait("tensor")
    wv.rwait("tensor")
    prog.wait("tensor", ld)
    for s in range(NS):
        bi = s % 2
        t0, t1 = s * TS, (s + 1) * TS
        xs[bi].load(rearr(xfm, t0, t1))
        xs[bi].rwait("tensor")
        for oc in range(CK):
            bk = pb[oc % 2]
            bk.acquire("tensor")
            for ci in range(CK):
                prog.op("tensor", lambda t, b=bk, oc=oc, ci=ci, bi=bi:
                        t.matmul(b.t[:, :],
                                 wk.t[:, ci, oc * 128:(oc + 1) * 128],
                                 xs[bi].t[:, ci, :],
                                 start=(ci == 0), stop=(ci == CK - 1)),
                        inc=[bk.wr_inc()] if ci == CK - 1 else None)
            prog.wait("scalar", bk.wr)
            incs = [bk.rd_inc()]
            if oc == CK - 1:
                incs.append(kg.wr_inc())
            prog.op("scalar", lambda sc, b=bk, oc=oc, t0=t0, t1=t1:
                    sc.activation(K_sb[:, oc, t0:t1], b.t[:, :], AF.Identity,
                                  bias=bk_col[:, oc:oc + 1]), inc=incs)
        bv_row.rwait("vector")
        for tc4 in range(4):
            tcg = s * 4 + tc4
            for ostr in range(2):
                bk = pb[2 + (tc4 * 2 + ostr) % 2]
                bk.acquire("tensor")
                for ci in range(CK):
                    prog.op("tensor",
                            lambda t, b=bk, ci=ci, tc4=tc4, ostr=ostr, bi=bi:
                            t.matmul(b.t[:, :],
                                     xs[bi].t[:, ci,
                                              tc4 * 128:(tc4 + 1) * 128],
                                     wv.t[:, ci, ostr * 512:(ostr + 1) * 512],
                                     start=(ci == 0), stop=(ci == CK - 1)),
                            inc=([bk.wr_inc(), xs[bi].rd_inc()]
                                 if (ci == CK - 1 and tc4 == 3 and ostr == 1)
                                 else ([bk.wr_inc()] if ci == CK - 1
                                       else None)))
                prog.wait("vector", bk.wr)
                dst = bass.AP(V_sb, tcg * NH * 65 + ostr * 8 * 65,
                              [[32 * NH * 65, 128], [65, 8], [1, 64]])
                incs = [bk.rd_inc()]
                if s == NS - 1 and tc4 == 3 and ostr == 1:
                    incs.append(bv_row.rd_inc())
                prog.op("vector", lambda v, b=bk, dst=dst, ostr=ostr:
                        v.tensor_tensor(
                            dst,
                            b.t[:, :].rearrange("p (a b) -> p a b", b=64),
                            bv_row.t[:, ostr * 512:(ostr + 1) * 512]
                            .rearrange("p (a b) -> p a b", b=64),
                            op=ALU.add), inc=incs)
        prog.wait("scalar", kg.wr)
        for ci in range(CK):
            sqkg.acquire("scalar")
            prog.op("scalar", lambda sc, ci=ci, t0=t0, t1=t1: sc.activation(
                sqk[:, :], K_sb[:, ci, t0:t1], AF.Square),
                inc=sqkg.wr_inc())
            prog.wait("tensor", sqkg.wr)
            for half in range(2):
                b0 = 64 * half
                bnk = nb[half]
                bnk.acquire("tensor")
                incs = [bnk.wr_inc()]
                if half == 1:
                    incs.append(sqkg.rd_inc())
                prog.op("tensor", lambda t, b=bnk, b0=b0:
                        t.matmul(b.t[:, :], ones_b[b0:b0 + 64, 0:128],
                                 sqk[b0:b0 + 64, :], start=True,
                                 stop=True), inc=incs)
                tmg.acquire("scalar")
                prog.wait("scalar", bnk.wr)
                prog.op("scalar", lambda sc, b=bnk: sc.activation(
                    tmpr[:, :], b.t[:, :], AF.Identity),
                    inc=[bnk.rd_inc(), tmg.wr_inc()])
                prog.wait("vector", tmg.wr)
                prog.op("vector", lambda v: v.tensor_tensor(
                    mxrow[:, :], mxrow[:, :], tmpr[:, :],
                    op=ALU.max), inc=tmg.rd_inc())
    w = TS
    while w > 1:
        h = w // 2
        prog.op("vector", lambda v, h=h, w=w: v.tensor_tensor(
            mxrow[:, 0:h], mxrow[:, 0:h], mxrow[:, w - h:w],
            op=ALU.max))
        w = h
    prog.op("vector", lambda v: v.tensor_copy(kmax[:, 1:2],
                                              mxrow[:, 0:1]))
    prog.drain_dmas()
    prog.flush("l0a_prep")
    sub.close()

    sub2 = ExitStack()
    wqo = LBuf(prog, "l0a_wqo", [128, CK, C], BF16, "gpsimd", sub2)

    def load_wqo(which):
        if wqo.rounds > 0:
            prog.wait("gpsimd", wqo.rd)
        for ci in range(CK):
            src_ap = (wqkv_h.ap()[ci * 128:(ci + 1) * 128, 0:C] if which == 0
                      else wout_h.ap()[ci * 128:(ci + 1) * 128, :])
            prog.dma("gpsimd", wqo.t[:, ci, :], src_ap, wqo.ld)
        wqo.ldv = wqo.ld.n
        wqo.rounds += 1

    load_wqo(0)
    xs2 = [LBuf(prog, f"l0a_y{i}", [128, CK, TS], BF16, "gpsimd", sub2)
           for i in range(1)]
    qs = sb(prog, "l0a_q", [128, CK, TS], BF16, sub2)
    qg = Guard(prog, "l0a_qg")
    sqq = sb(prog, "l0a_sqq", [128, TS], BF16, sub2)
    sqqg = Guard(prog, "l0a_sqqg")
    crow = sb(prog, "l0a_crow", [128, 2], F32, sub2)
    crg = Guard(prog, "l0a_crg")
    negC = sb(prog, "l0a_negC", [128, 1], F32, sub2)
    A_sb = sb(prog, "l0a_A", [128, CK, TS], BF16, sub2)
    Ag = Guard(prog, "l0a_Ag")
    dn = sb(prog, "l0a_dn", [1, 2, 512], F32, sub2)
    dng = [Guard(prog, f"l0a_dng{i}") for i in range(2)]
    rb = [sb(prog, f"l0a_rb{i}", [64, 512], F32, sub2) for i in range(2)]
    rbg = [Guard(prog, f"l0a_rbg{i}") for i in range(2)]
    ao = OBuf(prog, "l0a_ao", [128, CK, TS], F32, stack=sub2)
    SB = [Bank(prog, f"l0a_S{i}", 1024, stack=sub2) for i in range(2)]
    EB = [sb(prog, f"l0a_E{i}", [128, 1024], BF16, sub2) for i in range(2)]
    EG = [Guard(prog, f"l0a_EG{i}") for i in range(2)]
    EM = [prog.sem(f"l0a_EM{i}") for i in range(2)]
    PV = [Bank(prog, f"l0a_PV{i}", stack=sub2) for i in range(2)]
    WB = [Bank(prog, f"l0a_WB{i}", stack=sub2) for i in range(2)]
    nb2 = WB
    negC_wr = prog.sem("l0a_ncw")
    negC_rd = prog.sem("l0a_ncr")

    gi = 0
    for s in range(NS):
        bi = 0
        t0, t1 = s * TS, (s + 1) * TS
        if SHARED.get("overlap_prep"):
            SHARED["l1f_cast_idx"] += emit_l1f_cast(
                prog, io, SHARED["l1f_cast_idx"], SHARED["l1f_csem"])
        xs2[bi].load(rearr(xfm, t0, t1))
        xs2[bi].rwait("tensor")
        qg.acquire("scalar")
        wqo.rwait("tensor")
        for oc in range(CK):
            bk = nb2[oc % 2]
            bk.acquire("tensor")
            for ci in range(CK):
                incs = None
                if ci == CK - 1:
                    incs = [bk.wr_inc()]
                    if oc == CK - 1:
                        incs.append(xs2[bi].rd_inc())
                        incs.append(wqo.rd_inc())
                prog.op("tensor", lambda t, b=bk, oc=oc, ci=ci, bi=bi:
                        t.matmul(b.t[:, :],
                                 wqo.t[:, ci, oc * 128:(oc + 1) * 128],
                                 xs2[bi].t[:, ci, :],
                                 start=(ci == 0), stop=(ci == CK - 1)),
                        inc=incs)
            prog.wait("scalar", bk.wr)
            incs = [bk.rd_inc()] + ([qg.wr_inc()] if oc == CK - 1 else [])
            prog.op("scalar", lambda sc, b=bk, oc=oc: sc.activation(
                qs[:, oc, :], b.t[:, :], AF.Identity,
                bias=bq_col[:, oc:oc + 1]), inc=incs)
        load_wqo(1)
        crg.acquire("vector")
        prog.op("vector", lambda v: v.memset(mxrow[:, :], 0.0))
        prog.wait("scalar", qg.wr)
        for ci in range(CK):
            sqqg.acquire("scalar")
            prog.op("scalar", lambda sc, ci=ci: sc.activation(
                sqq[:, :], qs[:, ci, :], AF.Square), inc=sqqg.wr_inc())
            prog.wait("tensor", sqqg.wr)
            for half in range(2):
                b0 = 64 * half
                bnk = nb2[half]
                bnk.acquire("tensor")
                incs = [bnk.wr_inc()]
                if half == 1:
                    incs.append(sqqg.rd_inc())
                prog.op("tensor", lambda t, b=bnk, b0=b0:
                        t.matmul(b.t[:, :], ones_b[b0:b0 + 64, 0:128],
                                 sqq[b0:b0 + 64, :], start=True,
                                 stop=True), inc=incs)
                tmg.acquire("scalar")
                prog.wait("scalar", bnk.wr)
                prog.op("scalar", lambda sc, b=bnk: sc.activation(
                    tmpr[:, :], b.t[:, :], AF.Identity),
                    inc=[bnk.rd_inc(), tmg.wr_inc()])
                prog.wait("vector", tmg.wr)
                prog.op("vector", lambda v: v.tensor_tensor(
                    mxrow[:, :], mxrow[:, :], tmpr[:, :],
                    op=ALU.max), inc=tmg.rd_inc())
        w = TS
        while w > 1:
            h = w // 2
            prog.op("vector", lambda v, h=h, w=w: v.tensor_tensor(
                mxrow[:, 0:h], mxrow[:, 0:h],
                mxrow[:, w - h:w], op=ALU.max))
            w = h
        prog.op("vector", lambda v: v.tensor_tensor(
            crow[:, 0:1], mxrow[:, 0:1], kmax[:, 1:2],
            op=ALU.mult), inc=crg.wr_inc())
        prog.wait("scalar", crg.wr)
        prog.op("scalar", lambda sc: sc.activation(
            crow[:, 0:1], crow[:, 0:1], AF.Ln))
        prog.op("scalar", lambda sc: sc.activation(
            crow[:, 0:1], crow[:, 0:1], AF.Exp, scale=0.5),
            inc=crg.wr_inc())
        prog.wait("vector", crg.wr)
        prog.wait("vector", negC_rd)
        prog.op("vector", lambda v: v.tensor_scalar(
            negC[:, 0:1], crow[:, 0:1], -0.125, None, op0=ALU.mult),
            inc=[crg.wr_inc(), (negC_wr, 1)])
        prog.wait("scalar", negC_wr)
        Ag.acquire("vector")
        jlo = 4 * s
        groups = []
        for h in range(NH):
            for g0 in range(jlo, 32, 2):
                groups.append((h, g0, min(2, 32 - g0)))
        head_first = {}
        head_last = {}
        for idx, (h, g0, gn) in enumerate(groups):
            head_first.setdefault(h, idx)
            head_last[h] = idx

        def emit_S(idx):
            nonlocal gi
            h, g0, gn = groups[idx]
            ci, b0 = h // 2, 64 * (h % 2)
            e = gi % 2
            gi += 1
            sp, eb, eg = SB[e], EB[e], EG[e]
            sp.acquire("tensor")
            for jj in range(gn):
                prog.op("tensor",
                        lambda t, b=sp, jj=jj, jc=g0 + jj, ci=ci, b0=b0:
                        t.matmul(b.t[:, jj * 512:(jj + 1) * 512],
                                 K_sb[b0:b0 + 64, ci,
                                      jc * 128:(jc + 1) * 128],
                                 qs[b0:b0 + 64, ci, :],
                                 start=True, stop=True),
                        inc=[sp.wr_inc()] if jj == gn - 1 else None)
            eg.acquire("scalar")
            prog.wait("scalar", sp.wr)
            prog.op("scalar", lambda sc, sp=sp, eb=eb, gn=gn:
                    sc.activation(eb[:, 0:gn * 512], sp.t[:, 0:gn * 512],
                                  AF.Exp, bias=negC[:, 0:1], scale=0.125),
                    inc=[sp.rd_inc(), eg.wr_inc()])
            nmask = 0
            for jj in range(gn):
                jc = g0 + jj
                if jc < jlo + 4:
                    if nmask == 0:
                        prog.wait("vector", eg.wr)
                    nmask += 1
                    prog.op("vector",
                            lambda v, eb=eb, jj=jj, c4=jc - jlo:
                            v.tensor_tensor(
                                eb[:, jj * 512:(jj + 1) * 512],
                                eb[:, jj * 512:(jj + 1) * 512],
                                masks[:, c4, :], op=ALU.mult),
                            inc=(EM[e], 1))
            return e, nmask

        pend = {}
        pending_norm = [None]
        pend[0] = emit_S(0)
        for idx in range(len(groups)):
            if idx + 1 < len(groups):
                pend[idx + 1] = emit_S(idx + 1)
            h, g0, gn = groups[idx]
            ci, b0 = h // 2, 64 * (h % 2)
            e, nmask = pend.pop(idx)
            eb, eg = EB[e], EG[e]
            pv = PV[h % 2]
            if idx == head_first[h]:
                pv.acquire("tensor")
            prog.wait("tensor", eg.wr)
            if nmask:
                prog.wait("tensor", EM[e])
            for jj in range(gn):
                jc = g0 + jj
                lastj = (jc == 31)
                incs = [eg.rd_inc()] if jj == gn - 1 else []
                if lastj:
                    incs.append(pv.wr_inc())
                prog.op("tensor",
                        lambda t, b=pv, eb=eb, jj=jj, jc=jc, h=h,
                        fp=(idx == head_first[h] and jj == 0), lastj=lastj:
                        t.matmul(b.t[0:65, :],
                                 V_sb[:, jc, 65 * h:65 * h + 65],
                                 eb[:, jj * 512:(jj + 1) * 512],
                                 start=fp, stop=lastj),
                        inc=incs or None)
            if idx != head_last[h]:
                continue

            def norm_part1(h=h, pv=pv):
                dg = dng[h % 2]
                dg.acquire("scalar")
                prog.wait("scalar", pv.wr)
                prog.op("scalar", lambda sc, b=pv, h=h: sc.activation(
                    dn[0:1, h % 2, :], b.t[64:65, :], AF.Identity),
                    inc=dg.wr_inc())
                prog.wait("vector", dg.wr)
                prog.op("vector", lambda v, h=h: v.reciprocal(
                    dn[0:1, h % 2, :], dn[0:1, h % 2, :]))
                prog.op("vector", lambda v, h=h: v.tensor_copy(
                    dnb[0:1, h % 2, :], dn[0:1, h % 2, :]),
                    inc=dg.wr_inc())

            def norm_part2(h=h, ci=ci, b0=b0, pv=pv):
                dg = dng[h % 2]
                wbk2 = WB[h % 2]
                wbk2.acquire("tensor")
                prog.wait("tensor", dg.wr)
                prog.op("tensor", lambda t, b=wbk2, h=h: t.matmul(
                    b.t[:, :], e0b[:, :], dnb[0:128, h % 2, :], start=True,
                    stop=True), inc=[wbk2.wr_inc(), dg.rd_inc()])
                rbb, rbgb = rb[h % 2], rbg[h % 2]
                rbgb.acquire("vector")
                prog.wait("vector", wbk2.wr)
                prog.op("vector", lambda v, b=wbk2, rbb=rbb: v.tensor_copy(
                    rbb[0:64, :], b.t[0:64, :]),
                    inc=[wbk2.rd_inc(), rbgb.wr_inc()])
                prog.op("vector", lambda v, b=pv, rbb=rbb: v.tensor_tensor(
                    A_sb[b0:b0 + 64, ci, :], b.t[0:64, :],
                    rbb[0:64, :], op=ALU.mult),
                    inc=[pv.rd_inc(), rbgb.rd_inc(), Ag.wr_inc()])

            norm_part1()
            if pending_norm[0] is not None:
                pending_norm[0]()
            pending_norm[0] = norm_part2
        if pending_norm[0] is not None:
            pending_norm[0]()
            pending_norm[0] = None
        prog.wait("tensor", Ag.wr)
        wqo.rwait("tensor")
        ao.awrite("scalar")
        for oc in range(CK):
            wbk = WB[oc % 2]
            wbk.acquire("tensor")
            for ci2 in range(CK):
                incs = None
                if ci2 == CK - 1:
                    incs = [wbk.wr_inc()]
                    if oc == CK - 1:
                        incs.append(Ag.rd_inc())
                        incs.append(wqo.rd_inc())
                prog.op("tensor", lambda t, b=wbk, oc=oc, ci2=ci2: t.matmul(
                    b.t[:, :], wqo.t[:, ci2, oc * 128:(oc + 1) * 128],
                    A_sb[:, ci2, :], start=(ci2 == 0), stop=(ci2 == CK - 1)),
                    inc=incs)
            prog.wait("scalar", wbk.wr)
            incs = [wbk.rd_inc()]
            if oc == CK - 1:
                incs += [ao.wr_inc(), (negC_rd, 1)]
            prog.op("scalar", lambda sc, b=wbk, oc=oc: sc.activation(
                ao.t[:, oc, :], b.t[:, :], AF.Identity,
                bias=bo_col[:, oc:oc + 1]), inc=incs)
        ao.store(rearr(out_h, t0, t1))
        if s < NS - 1:
            load_wqo(0)
    prog.drain_dmas()
    prog.flush("l0a_main")
    sub2.close()



def phase_L1A(prog, io):
    nc = prog.nc
    x2 = io["x2"]
    wqkv_h, bqkv_h = io["L1_attn_Wqkv"], io["L1_attn_bqkv"]
    wln_h, bln_h = io["L1_attn_Wln"], io["L1_attn_bln"]
    out_h = io["laout"]

    TS = 512
    NS = S // TS
    wqkv = sb(prog, "l1a_wq", [128, CK, 3 * C], BF16)
    wln = sb(prog, "l1a_wl", [128, CK, C], BF16)
    bq_col = sb(prog, "l1a_bq", [128, CK], F32)
    bk_col = sb(prog, "l1a_bk", [128, CK], F32)
    bv_col = sb(prog, "l1a_bvc", [128, CK], F32)
    bln_col = sb(prog, "l1a_bln", [128, CK], F32)
    bv_row = LBuf(prog, "l1a_bvr", [128, C], F32, "gpsimd")
    kv_bf = sb(prog, "l1a_kvb", [128, CK, C], BF16)
    kvbg = Guard(prog, "l1a_kvbg")
    r2 = sb(prog, "l1a_r2", [128, CK], F32)
    scol = sb(prog, "l1a_scol", [128, CK], F32)
    eps24 = sb(prog, "l1a_e24", [128, 1], F32)
    ln64 = sb(prog, "l1a_l64", [128, 1], F32)
    epsr1 = sb(prog, "l1a_er1", [128, 1], F32)
    ones_b = sb(prog, "l1a_one", [128, 128], BF16)
    idb = sb(prog, "l1a_idb", [128, 128], BF16)
    ld = prog.sem("l1a_ld")

    prog.op("gpsimd", lambda g: g.memset(ones_b[:, :], 1.0))
    prog.op("gpsimd", lambda g: g.memset(eps24[:, :], 1e-24))
    prog.op("gpsimd", lambda g: g.memset(ln64[:, :], -float(np.log(64.0))))
    prog.op("gpsimd", lambda g: g.memset(epsr1[:, :], 1e-24))
    prog.op("gpsimd", lambda g: g.memset(r2[:, :], 0.0))
    idf32a = sb(prog, "l1a_idf32", [128, 128], F32)
    make_ident(prog, idf32a)
    prog.flush("l1a_id0")
    prog.op("vector", lambda v: v.tensor_copy(idb[:, :], idf32a[:, :]))
    for ci in range(CK):
        prog.dma("gpsimd", wqkv[:, ci, :],
                 wqkv_h.ap()[ci * 128:(ci + 1) * 128, :], ld)
        prog.dma("gpsimd", wln[:, ci, :],
                 wln_h.ap()[ci * 128:(ci + 1) * 128, :], ld)
    bias_cols_load(prog, bq_col, bqkv_h, CK, 0, ld)
    bias_cols_load(prog, bk_col, bqkv_h, CK, C, ld)
    bias_cols_load(prog, bv_col, bqkv_h, CK, 2 * C, ld)
    bias_cols_load(prog, bln_col, bln_h, CK, 0, ld)
    bv_row.load(bcast_ap(bqkv_h, 2 * C, C))
    if SHARED.get("overlap_prep") and "l1f_csem" in SHARED:
        tp_l1f = [sb(prog, "l1a_tpl1f", [128, 4 * C], BF16)]
        tpg_l1f = Guard(prog, "l1a_tpgl1f")
        tsem_l1f = prog.psem("l1f_tr")
        SHARED["l1f_tsem"] = tsem_l1f
        items_l1f = [(nm, ic) for nm, O, I in L1F_W
                     for ic in range(I // 128)]
        SHARED["_l1f_items"] = items_l1f
    prog.flush("l1a_init")

    sub = ExitStack()
    xs = [LBuf(prog, f"l1a_x{i}", [128, CK, TS], BF16, "gpsimd", sub)
          for i in range(2)]
    kf = sb(prog, "l1a_kf", [128, CK, TS], BF16, sub)
    kfg = Guard(prog, "l1a_kfg")
    k2 = sb(prog, "l1a_k2", [128, CK, TS], BF16, sub)
    k2g = Guard(prog, "l1a_k2g")
    k2a = sb(prog, "l1a_k2a", [128, CK, TS], BF16, sub)
    k2ag = Guard(prog, "l1a_k2ag")
    k2aT = sb(prog, "l1a_k2aT", [128, 4, C], BF16, sub)
    k2aTg = Guard(prog, "l1a_k2aTg")
    vtok = sb(prog, "l1a_vt", [128, 4, C], BF16, sub)
    vtg = Guard(prog, "l1a_vtg")
    kv_acc = sb(prog, "l1a_kv", [128, CK, C], F32, sub)
    stat = sb(prog, "l1a_st", [128, 2, TS], F32, sub)
    stg = Guard(prog, "l1a_stg")
    trash = sb(prog, "l1a_tr", [128, TS], BF16, sub)
    r2t = sb(prog, "l1a_r2t", [128, 1], F32, sub)
    r2tg = Guard(prog, "l1a_r2tg")
    kvg = Guard(prog, "l1a_kvg")
    pb = [Bank(prog, f"l1a_p{i}", stack=sub) for i in range(2)]
    nbk = Bank(prog, "l1a_n", stack=sub)
    tb = [Bank(prog, f"l1a_t{i}", 1024, BF16, stack=sub) for i in range(2)]
    kb = [Bank(prog, f"l1a_kb{i}", stack=sub) for i in range(2)]
    rs = prog.sem("l1a_rs")

    prog.op("gpsimd", lambda g: g.memset(kv_acc[:, :, :], 0.0))
    prog.wait("tensor", ld)
    for s in range(NS):
        bi = s % 2
        t0, t1 = s * TS, (s + 1) * TS
        if SHARED.get("l1f_tsem") is not None and SHARED["_l1f_items"]:
            take, SHARED["_l1f_items"] = (SHARED["_l1f_items"][:7],
                                          SHARED["_l1f_items"][7:])
            emit_l1f_transposes(prog, io, tp_l1f, tpg_l1f, take,
                                SHARED["l1f_csem"], SHARED["l1f_tsem"])
        xs[bi].load(rearr(x2, t0, t1))
        xs[bi].rwait("tensor")
        kfg.acquire("scalar")
        for oc in range(CK):
            bk = pb[oc % 2]
            bk.acquire("tensor")
            for ci in range(CK):
                prog.op("tensor", lambda t, b=bk, oc=oc, ci=ci, bi=bi:
                        t.matmul(b.t[:, :],
                                 wqkv[:, ci, C + oc * 128:C + (oc + 1) * 128],
                                 xs[bi].t[:, ci, :],
                                 start=(ci == 0), stop=(ci == CK - 1)),
                        inc=[bk.wr_inc()] if ci == CK - 1 else None)
            prog.wait("scalar", bk.wr)
            incs = [bk.rd_inc()] + ([kfg.wr_inc()] if oc == CK - 1 else [])
            prog.op("scalar", lambda sc, b=bk, oc=oc: sc.activation(
                kf[:, oc, :], b.t[:, :], AF.Identity,
                bias=bk_col[:, oc:oc + 1]), inc=incs)
        nbk.acquire("tensor")
        prog.wait("tensor", kfg.wr)
        for ci in range(CK):
            prog.op("tensor", lambda t, ci=ci: t.matmul(
                nbk.t[:, :], ones_b[:, :], kf[:, ci, :],
                start=(ci == 0), stop=(ci == CK - 1)),
                inc=[nbk.wr_inc()] if ci == CK - 1 else None)
        stg.acquire("vector")
        prog.wait("vector", nbk.wr)
        prog.op("vector", lambda v: v.tensor_scalar(
            stat[:, 0, :], nbk.t[:, :], 1.0 / C, None, op0=ALU.mult),
            inc=[nbk.rd_inc(), stg.wr_inc()])
        k2g.acquire("scalar")
        prog.wait("vector", stg.wr)
        for ci in range(CK):
            incs = [kfg.rd_inc(), stg.rd_inc()] if ci == CK - 1 else None
            prog.op("vector", lambda v, ci=ci: v.tensor_tensor(
                kf[:, ci, :], kf[:, ci, :], stat[:, 0, :],
                op=ALU.subtract), inc=incs)
        prog.wait("scalar", kfg.rd)
        for ci in range(CK):
            prog.op("scalar", lambda sc, ci=ci: sc.activation(
                k2[:, ci, :], kf[:, ci, :], AF.Square),
                inc=k2g.wr_inc() if ci == CK - 1 else None)
        nbk.acquire("tensor")
        prog.wait("tensor", k2g.wr)
        for ci in range(CK):
            prog.op("tensor", lambda t, ci=ci: t.matmul(
                nbk.t[:, :], ones_b[:, :], k2[:, ci, :],
                start=(ci == 0), stop=(ci == CK - 1)),
                inc=[nbk.wr_inc(), k2g.rd_inc()] if ci == CK - 1 else None)
        stg.acquire("vector")
        prog.wait("vector", nbk.wr)
        prog.op("vector", lambda v: v.tensor_scalar(
            stat[:, 1, :], nbk.t[:, :], 1e-7, None, op0=ALU.add),
            inc=nbk.rd_inc())
        prog.op("vector", lambda v: v.reciprocal(stat[:, 1, :],
                                                 stat[:, 1, :]),
                inc=stg.wr_inc())
        k2ag.acquire("vector")
        prog.wait("vector", stg.wr)
        for ci in range(CK):
            incs = [k2g.rd_inc(), stg.rd_inc(), k2ag.wr_inc()] \
                if ci == CK - 1 else None
            prog.op("vector", lambda v, ci=ci: v.tensor_tensor(
                k2a[:, ci, :], k2[:, ci, :], stat[:, 1, :], op=ALU.mult),
                inc=incs)
        prog.wait("scalar", k2ag.wr)
        for ci in range(CK):
            r2tg.acquire("scalar")
            prog.op("scalar", lambda sc, ci=ci: sc.activation(
                trash[:, :], k2a[:, ci, :], AF.Square,
                accum_out=r2t[:, 0:1]), inc=r2tg.wr_inc())
            prog.wait("vector", r2tg.wr)
            prog.op("vector", lambda v, ci=ci: v.tensor_tensor(
                r2[:, ci:ci + 1], r2[:, ci:ci + 1], r2t[:, 0:1], op=ALU.add),
                inc=r2tg.rd_inc())
        k2aTg.acquire("vector")
        prog.wait("tensor", k2ag.wr)
        for tc4 in range(4):
            for ci in range(CK):
                bk = tb[(tc4 * CK + ci) % 2]
                bk.acquire("tensor")
                incs = [bk.wr_inc()]
                if tc4 == 3 and ci == CK - 1:
                    incs.append(k2ag.rd_inc())
                prog.op("tensor", lambda t, b=bk, tc4=tc4, ci=ci: t.matmul(
                    b.t[:, 0:128], k2a[:, ci, tc4 * 128:(tc4 + 1) * 128],
                    idb[:, :], is_transpose=True), inc=incs)
                prog.wait("vector", bk.wr)
                incs = [bk.rd_inc()]
                if tc4 == 3 and ci == CK - 1:
                    incs.append(k2aTg.wr_inc())
                prog.op("vector", lambda v, b=bk, tc4=tc4, ci=ci:
                        v.tensor_copy(k2aT[:, tc4, ci * 128:(ci + 1) * 128],
                                      b.t[:, 0:128]), inc=incs)
        vtg.acquire("vector")
        bv_row.rwait("vector")
        for tc4 in range(4):
            for ostr in range(2):
                bk = pb[(tc4 * 2 + ostr) % 2]
                bk.acquire("tensor")
                for ci in range(CK):
                    prog.op("tensor",
                            lambda t, b=bk, ci=ci, tc4=tc4, ostr=ostr, bi=bi:
                            t.matmul(b.t[:, :],
                                     xs[bi].t[:, ci,
                                              tc4 * 128:(tc4 + 1) * 128],
                                     wqkv[:, ci,
                                          2 * C + ostr * 512:
                                          2 * C + (ostr + 1) * 512],
                                     start=(ci == 0), stop=(ci == CK - 1)),
                            inc=([bk.wr_inc(), xs[bi].rd_inc()]
                                 if (ci == CK - 1 and tc4 == 3 and ostr == 1)
                                 else ([bk.wr_inc()] if ci == CK - 1
                                       else None)))
                prog.wait("vector", bk.wr)
                incs = [bk.rd_inc()]
                if tc4 == 3 and ostr == 1:
                    incs.append(vtg.wr_inc())
                prog.op("vector", lambda v, b=bk, tc4=tc4, ostr=ostr:
                        v.tensor_tensor(
                            vtok[:, tc4, ostr * 512:(ostr + 1) * 512],
                            b.t[:, :],
                            bv_row.t[:, ostr * 512:(ostr + 1) * 512],
                            op=ALU.add), inc=incs)
        prog.wait("tensor", k2aTg.wr)
        prog.wait("tensor", vtg.wr)
        if s == 0:
            kvg.acquire("vector")
        for ci in range(CK):
            for ds in range(2):
                bk = kb[(ci * 2 + ds) % 2]
                bk.acquire("tensor")
                for tc4 in range(4):
                    incs = None
                    if tc4 == 3:
                        incs = [bk.wr_inc()]
                        if ci == CK - 1 and ds == 1:
                            incs += [k2aTg.rd_inc(), vtg.rd_inc()]
                    prog.op("tensor",
                            lambda t, b=bk, ci=ci, ds=ds, tc4=tc4: t.matmul(
                                b.t[:, :],
                                k2aT[:, tc4, ci * 128:(ci + 1) * 128],
                                vtok[:, tc4, ds * 512:(ds + 1) * 512],
                                start=(tc4 == 0), stop=(tc4 == 3)), inc=incs)
                prog.wait("vector", bk.wr)
                incs = [bk.rd_inc()]
                if s == NS - 1 and ci == CK - 1 and ds == 1:
                    incs.append(kvg.wr_inc())
                prog.op("vector", lambda v, b=bk, ci=ci, ds=ds:
                        v.tensor_tensor(
                            kv_acc[:, ci, ds * 512:(ds + 1) * 512],
                            kv_acc[:, ci, ds * 512:(ds + 1) * 512],
                            b.t[:, :], op=ALU.add), inc=incs)
    kvbg.acquire("vector")
    prog.wait("scalar", kvg.wr)
    prog.op("scalar", lambda sc: sc.activation(
        scol[:, 0:CK], r2[:, 0:CK], AF.Ln, bias=eps24[:, 0:1]))
    prog.op("scalar", lambda sc: sc.activation(
        scol[:, 0:CK], scol[:, 0:CK], AF.Exp, scale=-0.5,
        bias=ln64[:, 0:1]))
    prog.flush("l1a_scol")
    for ci in range(CK):
        incs = [kvbg.wr_inc()] if ci == CK - 1 else None
        prog.op("vector", lambda v, ci=ci: v.tensor_scalar(
            kv_bf[:, ci, :], kv_acc[:, ci, :], scol[:, ci:ci + 1], None,
            op0=ALU.mult), inc=incs)
    prog.drain_dmas()
    prog.flush("l1a_pass1")
    sub.close()

    sub2 = ExitStack()
    xs2 = [LBuf(prog, f"l1a_y{i}", [128, CK, TS], BF16, "gpsimd", sub2)
           for i in range(2)]
    qf = sb(prog, "l1a_qf", [128, CK, TS], BF16, sub2)
    qfg = Guard(prog, "l1a_qfg")
    q2 = sb(prog, "l1a_q2", [128, CK, TS], BF16, sub2)
    q2g = Guard(prog, "l1a_q2g")
    q2n = sb(prog, "l1a_q2n", [128, CK, TS], BF16, sub2)
    q2ng = Guard(prog, "l1a_q2ng")
    u_sb = sb(prog, "l1a_u", [128, CK, TS], BF16, sub2)
    ug = Guard(prog, "l1a_ug")
    stat2 = sb(prog, "l1a_st2", [128, 3, TS], F32, sub2)
    stg2 = Guard(prog, "l1a_stg2")
    ob = [OBuf(prog, f"l1a_o{i}", [128, CK, TS], F32, stack=sub2)
          for i in range(2)]
    pb2 = [Bank(prog, f"l1a_q{i}", stack=sub2) for i in range(2)]
    nb2 = Bank(prog, "l1a_n2", stack=sub2)
    rs2 = prog.sem("l1a_rs2")

    prog.wait("tensor", kvbg.wr)
    for s in range(NS):
        bi = s % 2
        t0, t1 = s * TS, (s + 1) * TS
        if SHARED.get("l1f_tsem") is not None and SHARED["_l1f_items"]:
            take, SHARED["_l1f_items"] = (SHARED["_l1f_items"][:7],
                                          SHARED["_l1f_items"][7:])
            emit_l1f_transposes(prog, io, tp_l1f, tpg_l1f, take,
                                SHARED["l1f_csem"], SHARED["l1f_tsem"])
        xs2[bi].load(rearr(x2, t0, t1))
        xs2[bi].rwait("tensor")
        qfg.acquire("scalar")
        for oc in range(CK):
            bk = pb2[oc % 2]
            bk.acquire("tensor")
            for ci in range(CK):
                prog.op("tensor", lambda t, b=bk, oc=oc, ci=ci, bi=bi:
                        t.matmul(b.t[:, :],
                                 wqkv[:, ci, oc * 128:(oc + 1) * 128],
                                 xs2[bi].t[:, ci, :],
                                 start=(ci == 0), stop=(ci == CK - 1)),
                        inc=[bk.wr_inc()] if ci == CK - 1 else None)
            prog.wait("scalar", bk.wr)
            incs = [bk.rd_inc()] + ([qfg.wr_inc()] if oc == CK - 1 else [])
            prog.op("scalar", lambda sc, b=bk, oc=oc: sc.activation(
                qf[:, oc, :], b.t[:, :], AF.Identity,
                bias=bq_col[:, oc:oc + 1]), inc=incs)
        nb2.acquire("tensor")
        prog.wait("tensor", qfg.wr)
        for ci in range(CK):
            prog.op("tensor", lambda t, ci=ci: t.matmul(
                nb2.t[:, :], ones_b[:, :], qf[:, ci, :],
                start=(ci == 0), stop=(ci == CK - 1)),
                inc=[nb2.wr_inc()] if ci == CK - 1 else None)
        stg2.acquire("vector")
        prog.wait("vector", nb2.wr)
        prog.op("vector", lambda v: v.tensor_scalar(
            stat2[:, 0, :], nb2.t[:, :], 1.0 / C, None, op0=ALU.mult),
            inc=[nb2.rd_inc(), stg2.wr_inc()])
        q2g.acquire("scalar")
        prog.wait("vector", stg2.wr)
        for ci in range(CK):
            incs = [qfg.rd_inc(), stg2.rd_inc()] if ci == CK - 1 else None
            prog.op("vector", lambda v, ci=ci: v.tensor_tensor(
                qf[:, ci, :], qf[:, ci, :], stat2[:, 0, :],
                op=ALU.subtract), inc=incs)
        prog.wait("scalar", qfg.rd)
        for ci in range(CK):
            prog.op("scalar", lambda sc, ci=ci: sc.activation(
                q2[:, ci, :], qf[:, ci, :], AF.Square),
                inc=q2g.wr_inc() if ci == CK - 1 else None)
        nb2.acquire("tensor")
        prog.wait("tensor", q2g.wr)
        for ci in range(CK):
            prog.op("tensor", lambda t, ci=ci: t.matmul(
                nb2.t[:, :], ones_b[:, :], q2[:, ci, :],
                start=(ci == 0), stop=(ci == CK - 1)),
                inc=[nb2.wr_inc(), q2g.rd_inc()] if ci == CK - 1 else None)
        stg2.acquire("vector")
        prog.wait("vector", nb2.wr)
        prog.op("vector", lambda v: v.tensor_scalar(
            stat2[:, 1, :], nb2.t[:, :], 1e-7, None, op0=ALU.add),
            inc=nb2.rd_inc())
        prog.op("vector", lambda v: v.reciprocal(stat2[:, 1, :],
                                                 stat2[:, 1, :]),
                inc=stg2.wr_inc())
        q2ng.acquire("vector")
        q2aw = prog.sem() if s == 0 else q2aw
        prog.wait("vector", stg2.wr)
        for ci in range(CK):
            incs = ([stg2.rd_inc(), (q2aw, 1), q2g.rd_inc()]
                    if ci == CK - 1 else None)
            prog.op("vector", lambda v, ci=ci: v.tensor_tensor(
                q2n[:, ci, :], q2[:, ci, :], stat2[:, 1, :], op=ALU.mult),
                inc=incs)
        prog.wait("scalar", q2aw)
        sq2d = prog.sem() if s == 0 else sq2d
        for ci in range(CK):
            incs = [(sq2d, 1)] if ci == CK - 1 else None
            prog.op("scalar", lambda sc, ci=ci: sc.activation(
                q2[:, ci, :], q2n[:, ci, :], AF.Square), inc=incs)
        nb2.acquire("tensor")
        prog.wait("tensor", sq2d)
        for ci in range(CK):
            prog.op("tensor", lambda t, ci=ci: t.matmul(
                nb2.t[:, :], ones_b[:, :], q2[:, ci, :],
                start=(ci == 0), stop=(ci == CK - 1)),
                inc=[nb2.wr_inc(), q2g.rd_inc()] if ci == CK - 1 else None)
        stg2.acquire("scalar")
        prog.wait("scalar", nb2.wr)
        prog.op("scalar", lambda sc: sc.activation(
            stat2[:, 2, :], nb2.t[:, :], AF.Ln, bias=epsr1[:, 0:1]),
            inc=nb2.rd_inc())
        prog.op("scalar", lambda sc: sc.activation(
            stat2[:, 2, :], stat2[:, 2, :], AF.Exp, scale=-0.5),
            inc=stg2.wr_inc())
        prog.wait("vector", stg2.wr)
        for ci in range(CK):
            incs = [stg2.rd_inc(), q2ng.wr_inc()] if ci == CK - 1 else None
            prog.op("vector", lambda v, ci=ci: v.tensor_tensor(
                q2n[:, ci, :], q2n[:, ci, :], stat2[:, 2, :], op=ALU.mult),
                inc=incs)
        ug.acquire("scalar")
        prog.wait("tensor", q2ng.wr)
        for dc in range(CK):
            bk = pb2[dc % 2]
            bk.acquire("tensor")
            for ci in range(CK):
                prog.op("tensor", lambda t, b=bk, dc=dc, ci=ci, bi=bi:
                        t.matmul(b.t[:, :],
                                 wqkv[:, ci,
                                      2 * C + dc * 128:2 * C + (dc + 1) * 128],
                                 xs2[bi].t[:, ci, :],
                                 start=(ci == 0), stop=False), inc=None)
            for ci in range(CK):
                incs = None
                if ci == CK - 1:
                    incs = [bk.wr_inc()]
                    if dc == CK - 1:
                        incs += [q2ng.rd_inc(), xs2[bi].rd_inc()]
                prog.op("tensor", lambda t, b=bk, dc=dc, ci=ci: t.matmul(
                    b.t[:, :], kv_bf[:, ci, dc * 128:(dc + 1) * 128],
                    q2n[:, ci, :], start=False, stop=(ci == CK - 1)),
                    inc=incs)
            prog.wait("scalar", bk.wr)
            incs = [bk.rd_inc()] + ([ug.wr_inc()] if dc == CK - 1 else [])
            prog.op("scalar", lambda sc, b=bk, dc=dc: sc.activation(
                u_sb[:, dc, :], b.t[:, :], AF.Identity,
                bias=bv_col[:, dc:dc + 1]), inc=incs)
        prog.wait("tensor", ug.wr)
        ob[bi].awrite("scalar")
        for oc in range(CK):
            bk = pb2[oc % 2]
            bk.acquire("tensor")
            for ci in range(CK):
                incs = None
                if ci == CK - 1:
                    incs = [bk.wr_inc()]
                    if oc == CK - 1:
                        incs.append(ug.rd_inc())
                prog.op("tensor", lambda t, b=bk, oc=oc, ci=ci: t.matmul(
                    b.t[:, :], wln[:, ci, oc * 128:(oc + 1) * 128],
                    u_sb[:, ci, :], start=(ci == 0), stop=(ci == CK - 1)),
                    inc=incs)
            prog.wait("scalar", bk.wr)
            incs = [bk.rd_inc()] + ([ob[bi].wr_inc()] if oc == CK - 1
                                    else [])
            prog.op("scalar", lambda sc, b=bk, oc=oc, bi=bi: sc.activation(
                ob[bi].t[:, oc, :], b.t[:, :], AF.Identity,
                bias=bln_col[:, oc:oc + 1]), inc=incs)
        ob[bi].store(rearr(out_h, t0, t1))
    prog.drain_dmas()
    prog.flush("l1a_pass2")
    sub2.close()



L1F_W = [("w1", 4 * C, C), ("w2", 4 * C, C), ("h1", 4 * C, 4 * C),
         ("h2", 4 * C, 4 * C), ("w3", C, 4 * C)]


def l1f_scratch(nc):
    if "l1f_wc" not in SHARED:
        wc, wt = {}, {}
        for nm, O, I in L1F_W:
            wc[nm] = nc.dram_tensor(f"l1f_{nm}c", [O, I], BF16,
                                    kind="Internal")
            wt[nm] = nc.dram_tensor(f"l1f_{nm}t", [I, O], BF16,
                                    kind="Internal")
        SHARED["l1f_wc"], SHARED["l1f_wt"] = wc, wt
    return SHARED["l1f_wc"], SHARED["l1f_wt"]


def l1f_cast_chunks(io):
    chunks = []
    for nm, O, I in L1F_W:
        nsp = 2 if O * I > 8 * 1024 * 1024 else 1
        step = O // nsp
        for i in range(nsp):
            chunks.append((nm, i * step, (i + 1) * step))
    return chunks


def emit_l1f_cast(prog, io, idx, sem):
    wc, _ = l1f_scratch(prog.nc)
    chunks = l1f_cast_chunks(io)
    if idx >= len(chunks):
        return 0
    nm, r0, r1 = chunks[idx]
    prog.op("gpsimd", lambda g, nm=nm, r0=r0, r1=r1: g.dma_start(
        wc[nm].ap()[r0:r1, :], io[f"L1_ffn_{nm}_w"].ap()[r0:r1, :],
        max_dma_last_dim=8192), inc=(sem, 16))
    return 1


def emit_l1f_transposes(prog, io, tp, tpg, items, csem, tsem):
    _, wt = l1f_scratch(prog.nc)
    wc = SHARED["l1f_wc"]
    sizes = dict((nm, (O, I)) for nm, O, I in L1F_W)
    for k, (nm, ic) in enumerate(items):
        O, I = sizes[nm]
        buf = tp[0]
        tpg.acquire("sync")
        prog.wait("sync", csem)
        prog.op("sync", lambda e, nm=nm, ic=ic, O=O, buf=buf: e.dma_start(
            buf[:, 0:O], wc[nm].ap()[0:O, ic * 128:(ic + 1) * 128],
            transpose=True), inc=(tpg.wr, 16))
        prog.wait("sync", tpg.wr)
        prog.op("sync", lambda e, nm=nm, ic=ic, O=O, buf=buf: e.dma_start(
            wt[nm].ap()[ic * 128:(ic + 1) * 128, 0:O], buf[:, 0:O]),
            inc=(tsem, 16))
        prog.wait("sync", tsem)
        prog.sig("sync", tpg.rd_inc())


def phase_L1F(prog, io):
    nc = prog.nc
    x3, out_h = io["x3"], io["f2out"]
    TS = 512
    NS = S // TS

    wc, wt = l1f_scratch(nc)
    prep_done = "l1f_tsem" in SHARED
    if not prep_done:
        csem = prog.sem("l1f_cast")
        for nm, O, I in L1F_W:
            prog.dma("gpsimd", wc[nm].ap(), io[f"L1_ffn_{nm}_w"].ap(), csem,
                     max_dma_last_dim=8192)
    s1c = sb(prog, "l1f_s1", [128, HK], F32)
    s2c = sb(prog, "l1f_s2", [128, HK], F32)
    h1c = sb(prog, "l1f_h1", [128, HK], F32)
    h2c = sb(prog, "l1f_h2", [128, HK], F32)
    s3c = sb(prog, "l1f_s3", [128, CK], F32)
    bias_cols_load(prog, s1c, io["L1_ffn_w1_s"], HK)
    bias_cols_load(prog, s2c, io["L1_ffn_w2_s"], HK)
    bias_cols_load(prog, h1c, io["L1_ffn_h1_s"], HK)
    bias_cols_load(prog, h2c, io["L1_ffn_h2_s"], HK)
    bias_cols_load(prog, s3c, io["L1_ffn_w3_s"], CK)
    prog.flush("l1f_init")

    if not prep_done:
        tp = [LBuf(prog, f"l1f_tp{i}", [128, 4 * C], BF16, "sync")
              for i in range(2)]
        tsem = prog.sem("l1f_tst")
        prog.wait("sync", csem)
        ti = 0
        for nm, O, I in L1F_W:
            for ic in range(I // 128):
                buf = tp[ti % 2]
                ti += 1
                buf.load(wc[nm].ap()[0:O, ic * 128:(ic + 1) * 128],
                         dst=buf.t[:, 0:O], transpose=True)
                buf.rwait("sync")
                prog.dma("sync", wt[nm].ap()[ic * 128:(ic + 1) * 128, 0:O],
                         buf.t[:, 0:O], tsem)
                prog.wait("sync", tsem)
                prog.sig("sync", buf.rd_inc())
        prog.wait("sync", tsem)
    else:
        ts_sh = SHARED["l1f_tsem"]
        prog.wait("sync", ts_sh)
        prog.wait("gpsimd", ts_sh)
    prog.flush("l1f_transp")

    if globals().get("DBG_L1F") == 1:
        dsem = prog.sem()
        prog.dma("gpsimd", out_h.ap(), wt["w1"].ap(), dsem)
        prog.drain_dmas()
        prog.flush("l1f_dbg")
        return

    xs = [LBuf(prog, f"l1f_x{i}", [128, CK, TS], BF16, "gpsimd")
          for i in range(2)]
    g_sb = sb(prog, "l1f_g", [128, HK, TS], BF16)
    gg = Guard(prog, "l1f_gg")
    u_sb = sb(prog, "l1f_u", [128, HK, TS], BF16)
    ug = Guard(prog, "l1f_ug")
    silb = sb(prog, "l1f_sil", [128, TS], BF16)
    silg = Guard(prog, "l1f_silg")
    y2b = sb(prog, "l1f_y2b", [128, TS], BF16)
    y2g = Guard(prog, "l1f_y2g")
    wg12 = [LBuf(prog, "l1f_w12g0", [128, CK, 2048], BF16, "sync")]
    hg = [LBuf(prog, f"l1f_hg{i}", [128, HK, 512], BF16, "sync")
          for i in range(2)]
    ob = [OBuf(prog, f"l1f_o{i}", [128, TS], F32) for i in range(2)]
    yb = [Bank(prog, f"l1f_yb{i}") for i in range(4)]
    hb = [Bank(prog, f"l1f_hb{i}") for i in range(2)]
    w3b = [Bank(prog, f"l1f_w3b{i}") for i in range(2)]

    def slab12(og):
        buf = wg12[0]
        ap1 = wt["w1"].ap().rearrange("(a p) o -> p a o", p=128)
        ap2 = wt["w2"].ap().rearrange("(a p) o -> p a o", p=128)
        if buf.rounds > 0:
            prog.wait(buf.eng, buf.rd)
        prog.dma(buf.eng, buf.t[:, :, 0:1024],
                 ap1[:, :, og * 1024:(og + 1) * 1024], buf.ld)
        prog.dma(buf.eng, buf.t[:, :, 1024:2048],
                 ap2[:, :, og * 1024:(og + 1) * 1024], buf.ld)
        buf.ldv = buf.ld.n
        buf.rounds += 1
        return buf

    def slabh(nm, og):
        buf = hg[og % 2]
        ap = wt[nm].ap().rearrange("(a p) o -> p a o", p=128)
        buf.load(ap[:, :, og * 512:(og + 1) * 512])
        return buf

    for s in range(NS):
        bi = s % 2
        t0, t1 = s * TS, (s + 1) * TS
        xs[bi].load(rearr(x3, t0, t1))
        xs[bi].rwait("tensor")
        gg.acquire("vector")
        for og in range(4):
            buf = slab12(og)
            buf.rwait("tensor")
            for oj in range(8):
                oc = og * 8 + oj
                b1 = yb[2 * (oj % 2)]
                b2 = yb[2 * (oj % 2) + 1]
                b1.acquire("tensor")
                b2.acquire("tensor")
                for ci in range(CK):
                    prog.op("tensor", lambda t, b=b1, buf=buf, oj=oj, ci=ci,
                            bi=bi: t.matmul(
                                b.t[:, :],
                                buf.t[:, ci, oj * 128:(oj + 1) * 128],
                                xs[bi].t[:, ci, :],
                                start=(ci == 0), stop=(ci == CK - 1)),
                            inc=[b1.wr_inc()] if ci == CK - 1 else None)
                for ci in range(CK):
                    incs = None
                    if ci == CK - 1:
                        incs = [b2.wr_inc()]
                        if oj == 7:
                            incs.append((buf.rd, 1))
                    prog.op("tensor", lambda t, b=b2, buf=buf, oj=oj, ci=ci,
                            bi=bi: t.matmul(
                                b.t[:, :],
                                buf.t[:, ci, 1024 + oj * 128:
                                      1024 + (oj + 1) * 128],
                                xs[bi].t[:, ci, :],
                                start=(ci == 0), stop=(ci == CK - 1)),
                            inc=incs)
                silg.acquire("scalar")
                prog.wait("scalar", b1.wr)
                prog.op("scalar", lambda sc, b=b1, oc=oc: sc.activation(
                    silb[:, :], b.t[:, :], AF.Silu,
                    scale=s1c[:, oc:oc + 1]),
                    inc=[b1.rd_inc(), silg.wr_inc()])
                y2g.acquire("vector")
                prog.wait("vector", b2.wr)
                prog.op("vector", lambda v, b=b2, oc=oc: v.tensor_scalar(
                    y2b[:, :], b.t[:, :], s2c[:, oc:oc + 1], None,
                    op0=ALU.mult), inc=[b2.rd_inc(), y2g.wr_inc()])
                prog.wait("vector", silg.wr)
                prog.op("vector", lambda v, oc=oc: v.tensor_tensor(
                    g_sb[:, oc, :], silb[:, :], y2b[:, :], op=ALU.mult),
                    inc=[silg.rd_inc(), y2g.rd_inc(), gg.wr_inc()])
        if globals().get("DBG_L1F") == 2 and s == 0:
            dsem = prog.sem("dbg2")
            prog.wait("gpsimd", gg.wr)
            for ic in range(CK):
                prog.dma("gpsimd", rearr(out_h, 0, TS)[:, ic, :],
                         g_sb[:, ic, :], dsem)
            prog.drain_dmas()
            prog.flush("l1f_dbg2")
            return
        ug.acquire("scalar")
        prog.wait("tensor", gg.wr)
        hbufs = [slabh("h1", 0)]
        for og in range(8):
            if og < 7:
                hbufs.append(slabh("h1", og + 1))
            buf = hbufs[og]
            buf.rwait("tensor")
            for oj in range(4):
                oc = og * 4 + oj
                bk = hb[oj % 2]
                bk.acquire("tensor")
                for ic in range(HK):
                    incs = None
                    if ic == HK - 1:
                        incs = [bk.wr_inc()]
                        if oj == 3:
                            incs.append((buf.rd, 1))
                        if oc == HK - 1:
                            incs.append(gg.rd_inc())
                    prog.op("tensor", lambda t, b=bk, buf=buf, oj=oj, ic=ic:
                            t.matmul(b.t[:, :],
                                     buf.t[:, ic, oj * 128:(oj + 1) * 128],
                                     g_sb[:, ic, :],
                                     start=(ic == 0), stop=(ic == HK - 1)),
                            inc=incs)
                prog.wait("scalar", bk.wr)
                incs = [bk.rd_inc()] + ([ug.wr_inc()])
                prog.op("scalar", lambda sc, b=bk, oc=oc: sc.activation(
                    u_sb[:, oc, :], b.t[:, :], AF.Gelu,
                    scale=h1c[:, oc:oc + 1]), inc=incs)
        gg.acquire("scalar")
        prog.wait("tensor", ug.wr)
        hbufs = [slabh("h2", 0)]
        for og in range(8):
            if og < 7:
                hbufs.append(slabh("h2", og + 1))
            buf = hbufs[og]
            buf.rwait("tensor")
            for oj in range(4):
                oc = og * 4 + oj
                bk = hb[oj % 2]
                bk.acquire("tensor")
                for ic in range(HK):
                    incs = None
                    if ic == HK - 1:
                        incs = [bk.wr_inc()]
                        if oj == 3:
                            incs.append((buf.rd, 1))
                        if oc == HK - 1:
                            incs.append(ug.rd_inc())
                    prog.op("tensor", lambda t, b=bk, buf=buf, oj=oj, ic=ic:
                            t.matmul(b.t[:, :],
                                     buf.t[:, ic, oj * 128:(oj + 1) * 128],
                                     u_sb[:, ic, :],
                                     start=(ic == 0), stop=(ic == HK - 1)),
                            inc=incs)
                prog.wait("scalar", bk.wr)
                incs = [bk.rd_inc()] + ([gg.wr_inc()])
                prog.op("scalar", lambda sc, b=bk, oc=oc: sc.activation(
                    g_sb[:, oc, :], b.t[:, :], AF.Gelu,
                    scale=h2c[:, oc:oc + 1]), inc=incs)
        prog.wait("tensor", gg.wr)
        hbufs = [slabh("w3", 0)]
        for og in range(2):
            if og < 1:
                hbufs.append(slabh("w3", og + 1))
            buf = hbufs[og]
            buf.rwait("tensor")
            for oj in range(4):
                oc = og * 4 + oj
                bk = w3b[oj % 2]
                bk.acquire("tensor")
                for ic in range(HK):
                    incs = None
                    if ic == HK - 1:
                        incs = [bk.wr_inc()]
                        if oj == 3:
                            incs.append((buf.rd, 1))
                        if oc == CK - 1:
                            incs.append(gg.rd_inc())
                            incs.append(xs[bi].rd_inc())
                    prog.op("tensor", lambda t, b=bk, buf=buf, oj=oj, ic=ic:
                            t.matmul(b.t[:, :],
                                     buf.t[:, ic, oj * 128:(oj + 1) * 128],
                                     g_sb[:, ic, :],
                                     start=(ic == 0), stop=(ic == HK - 1)),
                            inc=incs)
                obb = ob[oc % 2]
                obb.awrite("vector")
                prog.wait("vector", bk.wr)
                prog.op("vector", lambda v, b=bk, oc=oc, obb=obb:
                        v.tensor_scalar(obb.t[:, :], b.t[:, :],
                                        s3c[:, oc:oc + 1], None,
                                        op0=ALU.mult),
                        inc=[bk.rd_inc(), obb.wr_inc()])
                obb.store(rearr(out_h, t0, t1)[:, oc, :])



PHASE_IO = {
    "T": (["x_in"], ["xfm"]),
    "L0A": (["xfm", "L0_attn_w_qkv", "L0_attn_b_qkv", "L0_attn_w_out",
             "L0_attn_b_out"], ["attnout"]),
    "LN1": (["xfm", "attnout"], ["x1"]),
    "L0F": (["x1", "L0_ffn_weight", "L0_ffn_alpha", "L0_ffn_bias"],
            ["ffnout"]),
    "LN2": (["x1", "ffnout"], ["x2"]),
    "L1A": (["x2", "L1_attn_Wqkv", "L1_attn_bqkv", "L1_attn_Wln",
             "L1_attn_bln"], ["laout"]),
    "LN3": (["x2", "laout"], ["x3"]),
    "L1F": (["x3", "L1_ffn_w1_w", "L1_ffn_w1_s", "L1_ffn_w2_w",
             "L1_ffn_w2_s", "L1_ffn_h1_w", "L1_ffn_h1_s", "L1_ffn_h2_w",
             "L1_ffn_h2_s", "L1_ffn_w3_w", "L1_ffn_w3_s"], ["f2out"]),
    "LN4": (["x3", "f2out"], ["x4"]),
    "OUTT": (["x4"], ["out"]),
}

DRAM_SPECS = {
    "x_in": ([S, C], F32), "xfm": ([C, S], F32), "attnout": ([C, S], F32),
    "x1": ([C, S], F32), "ffnout": ([C, S], F32), "x2": ([C, S], F32),
    "laout": ([C, S], F32), "x3": ([C, S], F32), "f2out": ([C, S], F32),
    "x4": ([C, S], F32), "out": ([S, C], F32),
    "L0_attn_w_qkv": ([C, 3 * C], F32), "L0_attn_b_qkv": ([3 * C], F32),
    "L0_attn_w_out": ([C, C], F32), "L0_attn_b_out": ([C], F32),
    "L0_ffn_weight": ([C, C], F32), "L0_ffn_alpha": ([1], F32),
    "L0_ffn_bias": ([C], F32),
    "L1_attn_Wqkv": ([C, 3 * C], F32), "L1_attn_bqkv": ([3 * C], F32),
    "L1_attn_Wln": ([C, C], F32), "L1_attn_bln": ([C], F32),
    "L1_ffn_w1_w": ([4 * C, C], F32), "L1_ffn_w1_s": ([4 * C], F32),
    "L1_ffn_w2_w": ([4 * C, C], F32), "L1_ffn_w2_s": ([4 * C], F32),
    "L1_ffn_h1_w": ([4 * C, 4 * C], F32), "L1_ffn_h1_s": ([4 * C], F32),
    "L1_ffn_h2_w": ([4 * C, 4 * C], F32), "L1_ffn_h2_s": ([4 * C], F32),
    "L1_ffn_w3_w": ([C, 4 * C], F32), "L1_ffn_w3_s": ([C], F32),
}

PHASE_FN = {
    "T": phase_T,
    "OUTT": phase_OUTT,
    "L0A": phase_L0A,
    "L0F": phase_L0F,
    "L1A": phase_L1A,
    "L1F": phase_L1F,
    "LN1": lambda p, io: phase_LN(p, io, "xfm", "attnout", "x1", "ln1"),
    "LN2": lambda p, io: phase_LN(p, io, "x1", "ffnout", "x2", "ln2"),
    "LN3": lambda p, io: phase_LN(p, io, "x2", "laout", "x3", "ln3"),
    "LN4": lambda p, io: phase_LN(p, io, "x3", "f2out", "x4", "ln4"),
}


def build_group(phases, needed_later):
    SHARED.clear()
    nc = bass.Bass("TRN2", target_bir_lowering=False)
    if "L1F" in phases and "L0A" in phases:
        SHARED["overlap_prep"] = True
    reads, writes = set(), set()
    for ph in phases:
        r, w = PHASE_IO[ph]
        for t in r:
            if t not in writes:
                reads.add(t)
        writes.update(w)
    ext_in = sorted(reads)
    ext_out = sorted(w for w in writes if w in needed_later)
    handles = {}
    for t in ext_in:
        shp, dt = DRAM_SPECS[t]
        handles[t] = nc.dram_tensor(t, shp, dt, kind="ExternalInput")
    for t in sorted(writes):
        shp, dt = DRAM_SPECS[t]
        kind = "ExternalOutput" if t in ext_out else "Internal"
        handles[t] = nc.dram_tensor(t, shp, dt, kind=kind)
    for ph in phases:
        with ExitStack() as stack:
            prog = Prog(nc, stack)
            PHASE_FN[ph](prog, handles)
            prog.end_phase(ph)
    return nc, ext_in, ext_out


GROUPS = [["T", "L0A", "LN1", "L0F", "LN2", "L1A", "LN3", "L1F", "LN4",
           "OUTT"]]


def kernel(**inputs):
    vals = [dict() for _ in range(NCORE)]
    x = np.ascontiguousarray(np.asarray(inputs["L0_attn_x"],
                                        dtype=np.float32))
    for c in range(NCORE):
        vals[c]["x_in"] = x[c]
        for k, v in inputs.items():
            if k != "L0_attn_x":
                vals[c][k] = np.ascontiguousarray(np.asarray(v))
    for gi, phases in enumerate(GROUPS):
        future_reads = set()
        for phs in GROUPS[gi + 1:]:
            for ph in phs:
                future_reads |= set(PHASE_IO[ph][0])
        needed = future_reads | {"out"}
        nc, ins, outs = build_group(phases, needed)
        in_maps = [{t: vals[c][t] for t in ins} for c in range(NCORE)]
        res = run_bass_kernel_spmd(nc, in_maps, core_ids=list(range(NCORE)))
        for c in range(NCORE):
            for t in outs:
                vals[c][t] = np.asarray(res.results[c][t])
    return np.stack([vals[c]["out"] for c in range(NCORE)], axis=0)


# version 09
# speedup vs baseline: 1.0259x; 1.0259x over previous
import numpy as np
from contextlib import ExitStack

import concourse.bass as bass
import concourse.mybir as mybir
from concourse.bass_utils import run_bass_kernel_spmd

F32 = mybir.dt.float32
BF16 = mybir.dt.bfloat16
AF = mybir.ActivationFunctionType
ALU = mybir.AluOpType
AX = mybir.AxisListType

C = 1024
S = 4096
NH = 16
CK = C // 128
NCORE = 8
HID = 4096
HK = HID // 128

ENGS = ("tensor", "vector", "scalar", "gpsimd", "sync")

SHARED = {}



class Sem:
    __slots__ = ("h", "n")

    def __init__(self, h):
        self.h = h
        self.n = 0


class Prog:

    def __init__(self, nc, stack):
        self.nc = nc
        self.stack = stack
        self.q = {e: [] for e in ENGS}
        self._sid = 0
        self.sems = []
        self.dma_sems = {}

    _gsid = [0]

    def sem(self, name=None):
        Prog._gsid[0] += 1
        h = self.nc.alloc_semaphore(
            name=f"{name or 'sm'}_{Prog._gsid[0]}")
        s = Sem(h)
        self.sems.append(s)
        return s

    def op(self, eng, fn, inc=None):
        incs = [] if inc is None else ([inc] if isinstance(inc, tuple)
                                       else list(inc))
        for sem, k in incs:
            sem.n += k
        if incs:
            h0, k0 = incs[0][0].h, incs[0][1]
            self.q[eng].append(
                lambda e, fn=fn, h0=h0, k0=k0: fn(e).then_inc(h0, k0))
            if len(incs) > 1:
                v0 = incs[0][0].n
                self.q[eng].append(
                    lambda e, h0=h0, v0=v0: e.wait_ge(h0, v0))
                for sem, k in incs[1:]:
                    self.q[eng].append(
                        lambda e, h=sem.h, k=k: e.nop().then_inc(h, k))
        else:
            self.q[eng].append(fn)

    def sig(self, eng, inc):
        self.op(eng, lambda e: e.nop(), inc=inc)

    def wait(self, eng, sem, val=None):
        v = sem.n if val is None else val
        if v > 0:
            self.q[eng].append(lambda e, s=sem.h, v=v: e.wait_ge(s, v))

    def dma(self, eng, out, in_, sem, **kw):
        self.dma_sems[sem] = eng
        self.op(eng, lambda e, o=out, i=in_, kw=kw: e.dma_start(o, i, **kw),
                inc=(sem, 16))

    def drain_dmas(self):
        for sem, eng in self.dma_sems.items():
            self.wait(eng, sem)

    def flush(self, name=None):
        qs = {e: q for e, q in self.q.items() if q}
        self.q = {e: [] for e in ENGS}
        if not qs:
            return
        with self.nc.Block(name) as b:
            for e, q in qs.items():
                def player(eng, q=q):
                    for fn in q:
                        fn(eng)
                getattr(b, e)(player)

    def psem(self, name=None):
        Prog._gsid[0] += 1
        h = self.nc.alloc_semaphore(name=f"{name or 'ps'}_{Prog._gsid[0]}")
        return Sem(h)

    def end_phase(self, name):
        self.drain_dmas()
        self.flush(name + "_fin")
        self.nc.clear_and_free_semaphores([s.h for s in self.sems])
        self.sems = []
        self.dma_sems = {}


class Guard:

    def __init__(self, prog, name):
        self.p = prog
        self.wr = prog.sem(name + "w")
        self.rd = prog.sem(name + "r")
        self.written = False

    def acquire(self, eng):
        if self.written:
            self.p.wait(eng, self.rd)
        self.written = True

    def wr_inc(self):
        return (self.wr, 1)

    def rd_inc(self):
        return (self.rd, 1)


class Bank(Guard):
    def __init__(self, prog, name, width=512, dtype=F32, part=128, stack=None):
        super().__init__(prog, name)
        st = stack or prog.stack
        self.t = st.enter_context(
            prog.nc.psum_tensor(name, [part, width], dtype))


def sb(prog, name, shape, dtype, stack=None):
    st = stack or prog.stack
    return st.enter_context(prog.nc.sbuf_tensor(name, shape, dtype))


def full(t):
    n = len(t.shape)
    if n == 1:
        return t[:]
    if n == 2:
        return t[:, :]
    if n == 3:
        return t[:, :, :]
    return t[:, :, :, :]


class LBuf:

    def __init__(self, prog, name, shape, dtype, eng="gpsimd", stack=None):
        self.p = prog
        self.t = sb(prog, name, shape, dtype, stack)
        self.rd = prog.sem(name + "r")
        self.ld = prog.sem(name + "l")
        self.eng = eng
        self.rounds = 0
        self.ldv = 0

    def load(self, in_, dst=None, **kw):
        if self.rounds > 0:
            self.p.wait(self.eng, self.rd)
        self.p.dma(self.eng, full(self.t) if dst is None else dst, in_,
                   self.ld, **kw)
        self.ldv = self.ld.n
        self.rounds += 1

    def rwait(self, eng):
        self.p.wait(eng, self.ld, self.ldv)

    def rd_inc(self):
        return (self.rd, 1)


class OBuf:

    def __init__(self, prog, name, shape, dtype, eng="sync", stack=None):
        self.p = prog
        self.t = sb(prog, name, shape, dtype, stack)
        self.wr = prog.sem(name + "w")
        self.st = prog.sem(name + "s")
        self.eng = eng
        self.rounds = 0

    def awrite(self, eng):
        if self.rounds > 0:
            self.p.wait(eng, self.st, 16 * self.rounds)

    def wr_inc(self):
        return (self.wr, 1)

    def store(self, out_ap, src=None):
        self.p.wait(self.eng, self.wr)
        self.p.dma(self.eng, out_ap, full(self.t) if src is None else src,
                   self.st)
        self.rounds += 1


def rearr(handle, t0=None, t1=None):
    ap = handle.ap().rearrange("(a p) t -> p a t", p=128)
    if t0 is not None:
        ap = ap[:, :, t0:t1]
    return ap


def bcast_ap(handle, offset, n, parts=128):
    return bass.AP(handle, offset, [[0, parts], [1, n]])


def make_ident(prog, tile, n=128):
    prog.op("gpsimd", lambda g: g.memset(tile[:, :], 0.0))
    prog.op("gpsimd", lambda g: g.affine_select(
        out=tile[:, :], in_=tile[:, :], compare_op=ALU.not_equal, fill=1.0,
        base=0, pattern=[[-1, n]], channel_multiplier=1))


def make_e0(prog, name, stack=None):
    e0 = sb(prog, name, [128, 128], F32, stack)
    prog.op("gpsimd", lambda g: g.memset(e0[:, :], 0.0))
    prog.op("gpsimd", lambda g: g.memset(e0[0:1, :], 1.0))
    return e0


def bias_cols_load(prog, dst, src_handle, n_chunks, offset=0, sem=None):
    src = bass.AP(src_handle, offset, [[1, 128], [128, n_chunks]])
    s = sem or prog.sem()
    prog.dma("gpsimd", dst[:, 0:n_chunks], src, s,
             allow_slow_non_contiguous=True)
    return s


def roundtrip_bcast(prog, scr, row_src_ap, lbuf, bc_src_ap, sem, prior=()):
    for lb in prior:
        prog.wait("gpsimd", lb.ld)
    prog.dma("gpsimd", scr.ap(), row_src_ap, sem)
    prog.wait("gpsimd", sem)
    lbuf.load(bc_src_ap)



def phase_T(prog, io):
    x_in, xfm = io["x_in"], io["xfm"]
    idf = sb(prog, "t_idf", [128, 128], F32)
    xt = [LBuf(prog, f"t_xt{i}", [128, CK, 128], F32, "sync")
          for i in range(2)]
    ot = [OBuf(prog, f"t_ot{i}", [128, CK, 128], F32) for i in range(2)]
    banks = [Bank(prog, f"t_pb{i}", 512) for i in range(2)]

    make_ident(prog, idf)
    prog.flush("t_init")

    for tc in range(S // 128):
        bi = tc % 2
        xt[bi].load(x_in.ap()[tc * 128:(tc + 1) * 128, :]
                    .rearrange("p (a c) -> p a c", c=128))
        ot[bi].awrite("vector")
        for ci in range(CK):
            bk = banks[ci % 2]
            bk.acquire("tensor")
            if ci == 0:
                xt[bi].rwait("tensor")
            prog.op("tensor", lambda t, b=bk, s=xt[bi].t[:, ci, :]:
                    t.matmul(b.t[:, 0:128], s, idf[:, :], is_transpose=True),
                    inc=bk.wr_inc())
            prog.wait("vector", bk.wr)
            incs = [bk.rd_inc()]
            if ci == CK - 1:
                incs += [xt[bi].rd_inc(), ot[bi].wr_inc()]
            prog.op("vector", lambda v, b=bk, d=ot[bi].t[:, ci, :]:
                    v.tensor_copy(d, b.t[:, 0:128]), inc=incs)
        ot[bi].store(rearr(xfm, tc * 128, (tc + 1) * 128))



def phase_OUTT(prog, io):
    x4, out_h = io["x4"], io["out"]
    idf = sb(prog, "ou_idf", [128, 128], F32)
    xt = [LBuf(prog, f"ou_xt{i}", [128, CK, 128], F32, "sync")
          for i in range(2)]
    ot = [OBuf(prog, f"ou_ot{i}", [128, CK, 128], F32) for i in range(2)]
    banks = [Bank(prog, f"ou_pb{i}", 512) for i in range(2)]

    make_ident(prog, idf)
    prog.flush("ou_init")

    for tc in range(S // 128):
        bi = tc % 2
        xt[bi].load(rearr(x4, tc * 128, (tc + 1) * 128))
        ot[bi].awrite("vector")
        for ci in range(CK):
            bk = banks[ci % 2]
            bk.acquire("tensor")
            if ci == 0:
                xt[bi].rwait("tensor")
            prog.op("tensor", lambda t, b=bk, s=xt[bi].t[:, ci, :]:
                    t.matmul(b.t[:, 0:128], s, idf[:, :], is_transpose=True),
                    inc=bk.wr_inc())
            prog.wait("vector", bk.wr)
            incs = [bk.rd_inc()]
            if ci == CK - 1:
                incs += [xt[bi].rd_inc(), ot[bi].wr_inc()]
            prog.op("vector", lambda v, b=bk, d=ot[bi].t[:, ci, :]:
                    v.tensor_copy(d, b.t[:, 0:128]), inc=incs)
        ot[bi].store(out_h.ap()[tc * 128:(tc + 1) * 128, :]
                     .rearrange("p (a c) -> p a c", c=128))



def phase_LN(prog, io, a_name, b_name, o_name, tag):
    a_h, b_h, o_h = io[a_name], io[b_name], io[o_name]

    TS = 512
    af = [LBuf(prog, f"{tag}_a{i}", [128, CK, TS], F32, "sync")
          for i in range(2)]
    bf = [LBuf(prog, f"{tag}_b{i}", [128, CK, TS], F32, "sync")
          for i in range(2)]
    pre = [sb(prog, f"{tag}_p{i}", [128, CK, TS], F32) for i in range(2)]
    preg = [Guard(prog, f"{tag}_pg{i}") for i in range(2)]
    preb = sb(prog, f"{tag}_pb", [128, CK, TS], BF16)
    sqb = sb(prog, f"{tag}_sq", [128, CK, TS], BF16)
    pbg = Guard(prog, f"{tag}_pbg")
    sqg = Guard(prog, f"{tag}_sqg")
    ones = sb(prog, f"{tag}_ones", [128, 128], BF16)
    stat = sb(prog, f"{tag}_st", [128, 3, TS], F32)
    stg = Guard(prog, f"{tag}_stg")
    epsc = sb(prog, f"{tag}_eps", [128, 1], F32)
    ob = [OBuf(prog, f"{tag}_o{i}", [128, CK, TS], F32) for i in range(2)]
    bS = Bank(prog, f"{tag}_bS")
    bQ = Bank(prog, f"{tag}_bQ")

    prog.op("gpsimd", lambda g: g.memset(ones[:, :], 1.0))
    prog.op("gpsimd", lambda g: g.memset(epsc[:, :], 1e-5))
    prog.flush(f"{tag}_init")

    for s in range(S // TS):
        bi = s % 2
        t0, t1 = s * TS, (s + 1) * TS
        af[bi].load(rearr(a_h, t0, t1))
        bf[bi].load(rearr(b_h, t0, t1))
        preg[bi].acquire("vector")
        af[bi].rwait("vector")
        bf[bi].rwait("vector")
        for ci in range(CK):
            incs = ([preg[bi].wr_inc(), af[bi].rd_inc(), bf[bi].rd_inc()]
                    if ci == CK - 1 else None)
            prog.op("vector", lambda v, ci=ci, bi=bi: v.tensor_tensor(
                pre[bi][:, ci, :], af[bi].t[:, ci, :], bf[bi].t[:, ci, :],
                op=ALU.add), inc=incs)
        pbg.acquire("scalar")
        sqg.acquire("scalar")
        prog.wait("scalar", preg[bi].wr)
        for ci in range(CK):
            prog.op("scalar", lambda sc, ci=ci, bi=bi: sc.activation(
                preb[:, ci, :], pre[bi][:, ci, :], AF.Identity),
                inc=pbg.wr_inc() if ci == CK - 1 else None)
        for ci in range(CK):
            incs = [sqg.wr_inc()] if ci == CK - 1 else None
            if ci == CK - 1:
                incs.append(preg[bi].rd_inc())
            prog.op("scalar", lambda sc, ci=ci, bi=bi: sc.activation(
                sqb[:, ci, :], pre[bi][:, ci, :], AF.Square), inc=incs)
        bS.acquire("tensor")
        prog.wait("tensor", pbg.wr)
        for ci in range(CK):
            incs = [bS.wr_inc(), pbg.rd_inc()] if ci == CK - 1 else None
            prog.op("tensor", lambda t, ci=ci: t.matmul(
                bS.t[:, :], ones[:, :], preb[:, ci, :],
                start=(ci == 0), stop=(ci == CK - 1)), inc=incs)
        bQ.acquire("tensor")
        prog.wait("tensor", sqg.wr)
        for ci in range(CK):
            incs = [bQ.wr_inc(), sqg.rd_inc()] if ci == CK - 1 else None
            prog.op("tensor", lambda t, ci=ci: t.matmul(
                bQ.t[:, :], ones[:, :], sqb[:, ci, :],
                start=(ci == 0), stop=(ci == CK - 1)), inc=incs)
        stg.acquire("vector")
        prog.wait("vector", bS.wr)
        prog.wait("vector", bQ.wr)
        prog.op("vector", lambda v: v.tensor_scalar(
            stat[:, 0, :], bS.t[:, :], 1.0 / C, None, op0=ALU.mult),
            inc=bS.rd_inc())
        prog.op("vector", lambda v: v.tensor_scalar(
            stat[:, 1, :], bQ.t[:, :], 1.0 / C, None, op0=ALU.mult),
            inc=bQ.rd_inc())
        prog.op("vector", lambda v: v.tensor_tensor(
            stat[:, 2, :], stat[:, 0, :], stat[:, 0, :], op=ALU.mult))
        prog.op("vector", lambda v: v.tensor_tensor(
            stat[:, 1, :], stat[:, 1, :], stat[:, 2, :],
            op=ALU.subtract), inc=stg.wr_inc())
        prog.wait("scalar", stg.wr)
        prog.op("scalar", lambda sc: sc.activation(
            stat[:, 1, :], stat[:, 1, :], AF.Ln, bias=epsc[:, 0:1]))
        prog.op("scalar", lambda sc: sc.activation(
            stat[:, 2, :], stat[:, 1, :], AF.Exp, scale=-0.5),
            inc=stg.wr_inc())
        ob[bi].awrite("vector")
        prog.wait("vector", stg.wr)
        for ci in range(CK):
            prog.op("vector", lambda v, ci=ci, bi=bi: v.tensor_tensor(
                ob[bi].t[:, ci, :], pre[bi][:, ci, :], stat[:, 0, :],
                op=ALU.subtract))
            incs = None
            if ci == CK - 1:
                incs = [preg[bi].rd_inc(), stg.rd_inc(), ob[bi].wr_inc()]
            prog.op("vector", lambda v, ci=ci, bi=bi: v.tensor_tensor(
                ob[bi].t[:, ci, :], ob[bi].t[:, ci, :], stat[:, 2, :],
                op=ALU.mult), inc=incs)
        ob[bi].store(rearr(o_h, t0, t1))



def phase_L0F(prog, io):
    nc = prog.nc
    x1, w_h, al_h, bi_h, out_h = (io["x1"], io["L0_ffn_weight"],
                                  io["L0_ffn_alpha"], io["L0_ffn_bias"],
                                  io["ffnout"])
    drow_scr = nc.dram_tensor("l0f_drow", [2], F32, kind="Internal")

    wf = LBuf(prog, "l0f_wf", [128, CK, C], F32, "gpsimd")
    colsum = sb(prog, "l0f_cs", [128, CK], F32)
    csg = Guard(prog, "l0f_csg")
    tern = sb(prog, "l0f_tern", [128, CK, C], BF16)
    teg = Guard(prog, "l0f_teg")
    g01 = sb(prog, "l0f_g01", [128, C], BF16)
    sg = sb(prog, "l0f_sg", [128, C], BF16)
    sgg = Guard(prog, "l0f_sgg")
    wT = sb(prog, "l0f_wT", [128, CK, C], BF16)
    wTg = Guard(prog, "l0f_wTg")
    onesf = sb(prog, "l0f_onef", [128, 2], F32)
    row = sb(prog, "l0f_row", [1, 2], F32)
    rwg = Guard(prog, "l0f_rwg")
    dcol = LBuf(prog, "l0f_dcol", [128, 1], F32, "gpsimd")
    acol = LBuf(prog, "l0f_acol", [128, 1], F32, "gpsimd")
    bcol = sb(prog, "l0f_bcol", [128, CK], F32)
    idb = sb(prog, "l0f_idb", [128, 128], BF16)
    bS = Bank(prog, "l0f_bS")
    tb = [Bank(prog, f"l0f_tb{i}", 1024, BF16) for i in range(2)]
    mb = [Bank(prog, f"l0f_mb{i}") for i in range(2)]
    rs = prog.sem()

    idf32 = sb(prog, "l0f_idf32", [128, 128], F32)
    make_ident(prog, idf32)
    prog.flush("l0f_id0")
    prog.op("vector", lambda v: v.tensor_copy(idb[:, :], idf32[:, :]))
    prog.op("gpsimd", lambda g: g.memset(onesf[:, :], 1.0))
    wf.load(rearr(w_h))
    acol.load(bcast_ap(al_h, 0, 1))
    bias_cols_load(prog, bcol, bi_h, CK)
    prog.flush("l0f_init")

    csg.acquire("vector")
    wf.rwait("vector")
    for a in range(CK):
        prog.op("vector", lambda v, a=a: v.tensor_reduce(
            colsum[:, a:a + 1], wf.t[:, a, :], axis=AX.X, op=ALU.add,
            apply_absolute_value=True),
            inc=csg.wr_inc() if a == CK - 1 else None)
    bS.acquire("tensor")
    prog.wait("tensor", csg.wr)
    prog.op("tensor", lambda t: t.matmul(
        bS.t[0:2, 0:CK], onesf[:, 0:2], colsum[:, 0:CK],
        start=True, stop=True), inc=[bS.wr_inc(), csg.rd_inc()])
    rwg.acquire("vector")
    prog.wait("vector", bS.wr)
    csrow = sb(prog, "l0f_csrow", [2, CK], F32)
    trow = sb(prog, "l0f_trow", [1, CK], F32)
    prog.op("vector", lambda v: v.tensor_copy(
        csrow[0:2, 0:CK], bS.t[0:2, 0:CK]), inc=bS.rd_inc())
    prog.wait("scalar", bS.rd)
    prog.op("scalar", lambda sc: sc.activation(
        trow[0:1, 0:CK], csrow[0:1, 0:CK], AF.Identity,
        accum_out=row[0:1, 0:1]))
    prog.flush("l0f_delta0")
    prog.op("vector", lambda v: v.tensor_scalar(
        row[0:1, 1:2], row[0:1, 0:1], 0.7 / (C * C), None, op0=ALU.mult),
        inc=rwg.wr_inc())
    prog.wait("gpsimd", rwg.wr)
    prog.dma("gpsimd", drow_scr.ap()[0:1], row[0:1, 1:2], rs)
    prog.sig("gpsimd", rwg.rd_inc())
    prog.wait("gpsimd", rs)
    dcol.load(bcast_ap(drow_scr, 0, 1))
    teg.acquire("vector")
    dcol.rwait("vector")
    ab = sb(prog, "l0f_ab", [128, C], F32)
    for a in range(CK):
        sgg.acquire("scalar")
        prog.op("scalar", lambda sc, a=a: sc.activation(
            ab[:, :], wf.t[:, a, :], AF.Abs))
        prog.op("scalar", lambda sc, a=a: sc.activation(
            sg[:, :], wf.t[:, a, :], AF.Sign), inc=sgg.wr_inc())
        prog.wait("vector", sgg.wr)
        prog.op("vector", lambda v, a=a: v.tensor_scalar(
            g01[:, :], ab[:, :], dcol.t[:, 0:1], None, op0=ALU.is_gt))
        incs = [sgg.rd_inc()]
        if a == CK - 1:
            incs += [dcol.rd_inc(), wf.rd_inc(), teg.wr_inc()]
        prog.op("vector", lambda v, a=a: v.tensor_tensor(
            tern[:, a, :], g01[:, :], sg[:, :], op=ALU.mult), inc=incs)
    prog.flush("l0f_tern")

    if globals().get("DBG_L0F") == 1:
        dsem = prog.sem()
        for a in range(CK):
            prog.dma("gpsimd", rearr(out_h, 0, 1024)[:, a, :],
                     tern[:, a, :], dsem)
        prog.dma("gpsimd", rearr(out_h, 1024, 1025)[:, 1, :],
                 dcol.t[:, 0:1], dsem, allow_slow_non_contiguous=True)
        prog.dma("gpsimd", out_h.ap()[0:2, 1025:1025 + CK], csrow[0:2, 0:CK],
                 dsem)
        prog.dma("gpsimd", out_h.ap()[0:1, 1040:1042], row[0:1, 0:2], dsem)
        prog.drain_dmas()
        prog.flush("l0f_dbg")
        return

    wTg.acquire("vector")
    prog.wait("tensor", teg.wr)
    for a in range(CK):
        for icc in range(CK):
            bk = tb[(a * CK + icc) % 2]
            bk.acquire("tensor")
            prog.op("tensor", lambda t, b=bk, a=a, icc=icc: t.matmul(
                b.t[:, 0:128], tern[:, a, icc * 128:(icc + 1) * 128],
                idb[:, :], is_transpose=True), inc=bk.wr_inc())
            prog.wait("vector", bk.wr)
            last = (a == CK - 1 and icc == CK - 1)
            incs = [bk.rd_inc()] + ([wTg.wr_inc(), teg.rd_inc()] if last
                                    else [])
            prog.op("vector", lambda v, b=bk, a=a, icc=icc: v.tensor_copy(
                wT[:, icc, a * 128:(a + 1) * 128], b.t[:, 0:128]), inc=incs)
    prog.flush("l0f_transp")

    if globals().get("DBG_L0F") == 2:
        dsem = prog.sem()
        for ci in range(CK):
            prog.dma("gpsimd", rearr(out_h, 0, 1024)[:, ci, :],
                     wT[:, ci, :], dsem)
        prog.drain_dmas()
        prog.flush("l0f_dbg2")
        return

    xs = [LBuf(prog, f"l0f_x{i}", [128, CK, 512], BF16, "gpsimd")
          for i in range(2)]
    ob = [OBuf(prog, f"l0f_ob{i}", [128, CK, 512], F32) for i in range(2)]
    prog.wait("tensor", wTg.wr)
    acol.rwait("scalar")
    for s in range(S // 512):
        bi = s % 2
        t0, t1 = s * 512, (s + 1) * 512
        xs[bi].load(rearr(x1, t0, t1))
        xs[bi].rwait("tensor")
        ob[bi].awrite("scalar")
        for oc in range(CK):
            bk = mb[oc % 2]
            bk.acquire("tensor")
            for ci in range(CK):
                incs = None
                if ci == CK - 1:
                    incs = [bk.wr_inc()]
                    if oc == CK - 1:
                        incs.append(xs[bi].rd_inc())
                prog.op("tensor", lambda t, b=bk, oc=oc, ci=ci, bi=bi:
                        t.matmul(b.t[:, :],
                                 wT[:, ci, oc * 128:(oc + 1) * 128],
                                 xs[bi].t[:, ci, :],
                                 start=(ci == 0), stop=(ci == CK - 1)),
                        inc=incs)
            prog.wait("scalar", bk.wr)
            incs = [bk.rd_inc()] + ([ob[bi].wr_inc()] if oc == CK - 1 else [])
            prog.op("scalar", lambda sc, b=bk, oc=oc, bi=bi: sc.activation(
                ob[bi].t[:, oc, :], b.t[:, :], AF.Identity,
                bias=bcol[:, oc:oc + 1], scale=acol.t[:, 0:1]), inc=incs)
        ob[bi].store(rearr(out_h, t0, t1))



def phase_L0A(prog, io):
    nc = prog.nc
    xfm = io["xfm"]
    wqkv_h, bqkv_h = io["L0_attn_w_qkv"], io["L0_attn_b_qkv"]
    wout_h, bout_h = io["L0_attn_w_out"], io["L0_attn_b_out"]
    out_h = io["attnout"]

    TS = 512
    NS = S // TS
    K_sb = sb(prog, "l0a_K", [128, CK, S], BF16)
    V_sb = sb(prog, "l0a_V", [128, 32, NH * 65], BF16)
    bq_col = sb(prog, "l0a_bq", [128, CK], F32)
    bk_col = sb(prog, "l0a_bk", [128, CK], F32)
    bo_col = sb(prog, "l0a_bo", [128, CK], F32)
    masks = sb(prog, "l0a_msk", [128, 4, TS], BF16)
    ones_b = sb(prog, "l0a_one", [128, 128], BF16)
    mxrow = sb(prog, "l0a_mxr", [128, 512], F32)
    tmpr = sb(prog, "l0a_tmpr", [128, 512], BF16)
    tmg = Guard(prog, "l0a_tmg")
    kmax = sb(prog, "l0a_kmax", [128, 2], F32)
    dnb = sb(prog, "l0a_dnb", [128, 2, 512], BF16)
    e0b = sb(prog, "l0a_e0b", [128, 128], BF16)
    ld = prog.sem("l0a_ld")

    prog.op("gpsimd", lambda g: g.memset(ones_b[:, :], 1.0))
    prog.op("gpsimd", lambda g: g.memset(mxrow[:, :], 0.0))
    prog.op("gpsimd", lambda g: g.memset(masks[:, :, :], 1.0))
    prog.op("gpsimd", lambda g: g.memset(dnb[:, :, :], 0.0))
    prog.op("gpsimd", lambda g: g.memset(e0b[:, :], 0.0))
    prog.op("gpsimd", lambda g: g.memset(e0b[0:1, :], 1.0))
    for c4 in range(4):
        prog.op("gpsimd", lambda g, c4=c4: g.affine_select(
            out=masks[:, c4, :], in_=masks[:, c4, :], compare_op=ALU.is_ge,
            fill=0.0, base=128 * c4, pattern=[[-1, TS]],
            channel_multiplier=1))
    prog.op("gpsimd", lambda g: g.memset(
        bass.AP(V_sb, 64, [[32 * NH * 65, 128], [NH * 65, 32], [65, NH]]),
        1.0))
    bias_cols_load(prog, bq_col, bqkv_h, CK, 0, ld)
    bias_cols_load(prog, bk_col, bqkv_h, CK, C, ld)
    bias_cols_load(prog, bo_col, bout_h, CK, 0, ld)
    if SHARED.get("overlap_prep"):
        csem = prog.psem("l1f_cast")
        SHARED["l1f_csem"] = csem
        SHARED["l1f_cast_idx"] = 0
        for _ in range(2):
            SHARED["l1f_cast_idx"] += emit_l1f_cast(
                prog, io, SHARED["l1f_cast_idx"], csem)
    prog.flush("l0a_init")

    sub = ExitStack()
    wk = LBuf(prog, "l0a_wk", [128, CK, C], BF16, "gpsimd", sub)
    wv = LBuf(prog, "l0a_wv", [128, CK, C], BF16, "gpsimd", sub)
    bv_row = LBuf(prog, "l0a_bv", [128, C], F32, "gpsimd", sub)
    for ci in range(CK):
        prog.dma("gpsimd", wk.t[:, ci, :],
                 wqkv_h.ap()[ci * 128:(ci + 1) * 128, C:2 * C], wk.ld)
        prog.dma("gpsimd", wv.t[:, ci, :],
                 wqkv_h.ap()[ci * 128:(ci + 1) * 128, 2 * C:3 * C], wv.ld)
    wk.ldv, wk.rounds = wk.ld.n, 1
    wv.ldv, wv.rounds = wv.ld.n, 1
    bv_row.load(bcast_ap(bqkv_h, 2 * C, C))
    xs = [LBuf(prog, f"l0a_x{i}", [128, CK, TS], BF16, "gpsimd", sub)
          for i in range(2)]
    sqk = sb(prog, "l0a_sqk", [128, TS], BF16, sub)
    sqkg = Guard(prog, "l0a_sqkg")
    kg = Guard(prog, "l0a_kg")
    pb = [Bank(prog, f"l0a_pp{i}", stack=sub) for i in range(4)]
    nb = [Bank(prog, f"l0a_nb{i}", stack=sub) for i in range(2)]

    wk.rwait("tensor")
    wv.rwait("tensor")
    prog.wait("tensor", ld)
    for s in range(NS):
        bi = s % 2
        t0, t1 = s * TS, (s + 1) * TS
        xs[bi].load(rearr(xfm, t0, t1))
        xs[bi].rwait("tensor")
        for oc in range(CK):
            bk = pb[oc % 2]
            bk.acquire("tensor")
            for ci in range(CK):
                prog.op("tensor", lambda t, b=bk, oc=oc, ci=ci, bi=bi:
                        t.matmul(b.t[:, :],
                                 wk.t[:, ci, oc * 128:(oc + 1) * 128],
                                 xs[bi].t[:, ci, :],
                                 start=(ci == 0), stop=(ci == CK - 1)),
                        inc=[bk.wr_inc()] if ci == CK - 1 else None)
            prog.wait("scalar", bk.wr)
            incs = [bk.rd_inc()]
            if oc == CK - 1:
                incs.append(kg.wr_inc())
            prog.op("scalar", lambda sc, b=bk, oc=oc, t0=t0, t1=t1:
                    sc.activation(K_sb[:, oc, t0:t1], b.t[:, :], AF.Identity,
                                  bias=bk_col[:, oc:oc + 1]), inc=incs)
        bv_row.rwait("vector")
        for tc4 in range(4):
            tcg = s * 4 + tc4
            for ostr in range(2):
                bk = pb[2 + (tc4 * 2 + ostr) % 2]
                bk.acquire("tensor")
                for ci in range(CK):
                    prog.op("tensor",
                            lambda t, b=bk, ci=ci, tc4=tc4, ostr=ostr, bi=bi:
                            t.matmul(b.t[:, :],
                                     xs[bi].t[:, ci,
                                              tc4 * 128:(tc4 + 1) * 128],
                                     wv.t[:, ci, ostr * 512:(ostr + 1) * 512],
                                     start=(ci == 0), stop=(ci == CK - 1)),
                            inc=([bk.wr_inc(), xs[bi].rd_inc()]
                                 if (ci == CK - 1 and tc4 == 3 and ostr == 1)
                                 else ([bk.wr_inc()] if ci == CK - 1
                                       else None)))
                prog.wait("vector", bk.wr)
                dst = bass.AP(V_sb, tcg * NH * 65 + ostr * 8 * 65,
                              [[32 * NH * 65, 128], [65, 8], [1, 64]])
                incs = [bk.rd_inc()]
                if s == NS - 1 and tc4 == 3 and ostr == 1:
                    incs.append(bv_row.rd_inc())
                prog.op("vector", lambda v, b=bk, dst=dst, ostr=ostr:
                        v.tensor_tensor(
                            dst,
                            b.t[:, :].rearrange("p (a b) -> p a b", b=64),
                            bv_row.t[:, ostr * 512:(ostr + 1) * 512]
                            .rearrange("p (a b) -> p a b", b=64),
                            op=ALU.add), inc=incs)
        prog.wait("scalar", kg.wr)
        for ci in range(CK):
            sqkg.acquire("scalar")
            prog.op("scalar", lambda sc, ci=ci, t0=t0, t1=t1: sc.activation(
                sqk[:, :], K_sb[:, ci, t0:t1], AF.Square),
                inc=sqkg.wr_inc())
            prog.wait("tensor", sqkg.wr)
            for half in range(2):
                b0 = 64 * half
                bnk = nb[half]
                bnk.acquire("tensor")
                incs = [bnk.wr_inc()]
                if half == 1:
                    incs.append(sqkg.rd_inc())
                prog.op("tensor", lambda t, b=bnk, b0=b0:
                        t.matmul(b.t[:, :], ones_b[b0:b0 + 64, 0:128],
                                 sqk[b0:b0 + 64, :], start=True,
                                 stop=True), inc=incs)
                tmg.acquire("scalar")
                prog.wait("scalar", bnk.wr)
                prog.op("scalar", lambda sc, b=bnk: sc.activation(
                    tmpr[:, :], b.t[:, :], AF.Identity),
                    inc=[bnk.rd_inc(), tmg.wr_inc()])
                prog.wait("vector", tmg.wr)
                prog.op("vector", lambda v: v.tensor_tensor(
                    mxrow[:, :], mxrow[:, :], tmpr[:, :],
                    op=ALU.max), inc=tmg.rd_inc())
    w = TS
    while w > 1:
        h = w // 2
        prog.op("vector", lambda v, h=h, w=w: v.tensor_tensor(
            mxrow[:, 0:h], mxrow[:, 0:h], mxrow[:, w - h:w],
            op=ALU.max))
        w = h
    prog.op("vector", lambda v: v.tensor_copy(kmax[:, 1:2],
                                              mxrow[:, 0:1]))
    prog.drain_dmas()
    prog.flush("l0a_prep")
    sub.close()

    sub2 = ExitStack()
    wqo = LBuf(prog, "l0a_wqo", [128, CK, C], BF16, "gpsimd", sub2)

    def load_wqo(which):
        if wqo.rounds > 0:
            prog.wait("gpsimd", wqo.rd)
        for ci in range(CK):
            src_ap = (wqkv_h.ap()[ci * 128:(ci + 1) * 128, 0:C] if which == 0
                      else wout_h.ap()[ci * 128:(ci + 1) * 128, :])
            prog.dma("gpsimd", wqo.t[:, ci, :], src_ap, wqo.ld)
        wqo.ldv = wqo.ld.n
        wqo.rounds += 1

    load_wqo(0)
    xs2 = [LBuf(prog, f"l0a_y{i}", [128, CK, TS], BF16, "gpsimd", sub2)
           for i in range(1)]
    qs = sb(prog, "l0a_q", [128, CK, TS], BF16, sub2)
    qg = Guard(prog, "l0a_qg")
    sqq = sb(prog, "l0a_sqq", [128, TS], BF16, sub2)
    sqqg = Guard(prog, "l0a_sqqg")
    crow = sb(prog, "l0a_crow", [128, 2], F32, sub2)
    crg = Guard(prog, "l0a_crg")
    negC = sb(prog, "l0a_negC", [128, 1], F32, sub2)
    A_sb = sb(prog, "l0a_A", [128, CK, TS], BF16, sub2)
    Ag = Guard(prog, "l0a_Ag")
    dn = sb(prog, "l0a_dn", [1, 2, 512], F32, sub2)
    dng = [Guard(prog, f"l0a_dng{i}") for i in range(2)]
    rb = [sb(prog, f"l0a_rb{i}", [128, 512], F32, sub2) for i in range(2)]
    rbg = [Guard(prog, f"l0a_rbg{i}") for i in range(2)]
    ao = OBuf(prog, "l0a_ao", [128, CK, TS], F32, stack=sub2)
    SB = [Bank(prog, f"l0a_S{i}", 1024, stack=sub2) for i in range(2)]
    EB = [sb(prog, f"l0a_E{i}", [128, 1024], BF16, sub2) for i in range(2)]
    EG = [Guard(prog, f"l0a_EG{i}") for i in range(2)]
    EM = [prog.sem(f"l0a_EM{i}") for i in range(2)]
    PV = [Bank(prog, f"l0a_PV{i}", stack=sub2) for i in range(2)]
    WB = [Bank(prog, f"l0a_WB{i}", stack=sub2) for i in range(2)]
    nb2 = WB
    negC_wr = prog.sem("l0a_ncw")
    negC_rd = prog.sem("l0a_ncr")

    gi = 0
    for s in range(NS):
        bi = 0
        t0, t1 = s * TS, (s + 1) * TS
        if SHARED.get("overlap_prep"):
            SHARED["l1f_cast_idx"] += emit_l1f_cast(
                prog, io, SHARED["l1f_cast_idx"], SHARED["l1f_csem"])
        xs2[bi].load(rearr(xfm, t0, t1))
        xs2[bi].rwait("tensor")
        qg.acquire("scalar")
        wqo.rwait("tensor")
        for oc in range(CK):
            bk = nb2[oc % 2]
            bk.acquire("tensor")
            for ci in range(CK):
                incs = None
                if ci == CK - 1:
                    incs = [bk.wr_inc()]
                    if oc == CK - 1:
                        incs.append(xs2[bi].rd_inc())
                        incs.append(wqo.rd_inc())
                prog.op("tensor", lambda t, b=bk, oc=oc, ci=ci, bi=bi:
                        t.matmul(b.t[:, :],
                                 wqo.t[:, ci, oc * 128:(oc + 1) * 128],
                                 xs2[bi].t[:, ci, :],
                                 start=(ci == 0), stop=(ci == CK - 1)),
                        inc=incs)
            prog.wait("scalar", bk.wr)
            incs = [bk.rd_inc()] + ([qg.wr_inc()] if oc == CK - 1 else [])
            prog.op("scalar", lambda sc, b=bk, oc=oc: sc.activation(
                qs[:, oc, :], b.t[:, :], AF.Identity,
                bias=bq_col[:, oc:oc + 1]), inc=incs)
        load_wqo(1)
        crg.acquire("vector")
        prog.op("vector", lambda v: v.memset(mxrow[:, :], 0.0))
        prog.wait("scalar", qg.wr)
        for ci in range(CK):
            sqqg.acquire("scalar")
            prog.op("scalar", lambda sc, ci=ci: sc.activation(
                sqq[:, :], qs[:, ci, :], AF.Square), inc=sqqg.wr_inc())
            prog.wait("tensor", sqqg.wr)
            for half in range(2):
                b0 = 64 * half
                bnk = nb2[half]
                bnk.acquire("tensor")
                incs = [bnk.wr_inc()]
                if half == 1:
                    incs.append(sqqg.rd_inc())
                prog.op("tensor", lambda t, b=bnk, b0=b0:
                        t.matmul(b.t[:, :], ones_b[b0:b0 + 64, 0:128],
                                 sqq[b0:b0 + 64, :], start=True,
                                 stop=True), inc=incs)
                tmg.acquire("scalar")
                prog.wait("scalar", bnk.wr)
                prog.op("scalar", lambda sc, b=bnk: sc.activation(
                    tmpr[:, :], b.t[:, :], AF.Identity),
                    inc=[bnk.rd_inc(), tmg.wr_inc()])
                prog.wait("vector", tmg.wr)
                prog.op("vector", lambda v: v.tensor_tensor(
                    mxrow[:, :], mxrow[:, :], tmpr[:, :],
                    op=ALU.max), inc=tmg.rd_inc())
        w = TS
        while w > 1:
            h = w // 2
            prog.op("vector", lambda v, h=h, w=w: v.tensor_tensor(
                mxrow[:, 0:h], mxrow[:, 0:h],
                mxrow[:, w - h:w], op=ALU.max))
            w = h
        prog.op("vector", lambda v: v.tensor_tensor(
            crow[:, 0:1], mxrow[:, 0:1], kmax[:, 1:2],
            op=ALU.mult), inc=crg.wr_inc())
        prog.wait("scalar", crg.wr)
        prog.op("scalar", lambda sc: sc.activation(
            crow[:, 0:1], crow[:, 0:1], AF.Ln))
        prog.op("scalar", lambda sc: sc.activation(
            crow[:, 0:1], crow[:, 0:1], AF.Exp, scale=0.5),
            inc=crg.wr_inc())
        prog.wait("vector", crg.wr)
        prog.wait("vector", negC_rd)
        prog.op("vector", lambda v: v.tensor_scalar(
            negC[:, 0:1], crow[:, 0:1], -0.125, None, op0=ALU.mult),
            inc=[crg.wr_inc(), (negC_wr, 1)])
        prog.wait("scalar", negC_wr)
        Ag.acquire("vector")
        jlo = 4 * s
        groups = []
        for h in range(NH):
            for g0 in range(jlo, 32, 2):
                groups.append((h, g0, min(2, 32 - g0)))
        head_first = {}
        head_last = {}
        for idx, (h, g0, gn) in enumerate(groups):
            head_first.setdefault(h, idx)
            head_last[h] = idx

        def emit_S(idx):
            nonlocal gi
            h, g0, gn = groups[idx]
            ci, b0 = h // 2, 64 * (h % 2)
            e = gi % 2
            gi += 1
            sp, eb, eg = SB[e], EB[e], EG[e]
            sp.acquire("tensor")
            for jj in range(gn):
                prog.op("tensor",
                        lambda t, b=sp, jj=jj, jc=g0 + jj, ci=ci, b0=b0:
                        t.matmul(b.t[:, jj * 512:(jj + 1) * 512],
                                 K_sb[b0:b0 + 64, ci,
                                      jc * 128:(jc + 1) * 128],
                                 qs[b0:b0 + 64, ci, :],
                                 start=True, stop=True),
                        inc=[sp.wr_inc()] if jj == gn - 1 else None)
            eg.acquire("scalar")
            prog.wait("scalar", sp.wr)
            prog.op("scalar", lambda sc, sp=sp, eb=eb, gn=gn:
                    sc.activation(eb[:, 0:gn * 512], sp.t[:, 0:gn * 512],
                                  AF.Exp, bias=negC[:, 0:1], scale=0.125),
                    inc=[sp.rd_inc(), eg.wr_inc()])
            nmask = 0
            for jj in range(gn):
                jc = g0 + jj
                if jc < jlo + 4:
                    if nmask == 0:
                        prog.wait("vector", eg.wr)
                    nmask += 1
                    prog.op("vector",
                            lambda v, eb=eb, jj=jj, c4=jc - jlo:
                            v.tensor_tensor(
                                eb[:, jj * 512:(jj + 1) * 512],
                                eb[:, jj * 512:(jj + 1) * 512],
                                masks[:, c4, :], op=ALU.mult),
                            inc=(EM[e], 1))
            return e, nmask

        pend = {}
        pending_norm = [None]
        pend[0] = emit_S(0)
        for idx in range(len(groups)):
            if idx + 1 < len(groups):
                pend[idx + 1] = emit_S(idx + 1)
            h, g0, gn = groups[idx]
            ci, b0 = h // 2, 64 * (h % 2)
            e, nmask = pend.pop(idx)
            eb, eg = EB[e], EG[e]
            pv = PV[h % 2]
            if idx == head_first[h]:
                pv.acquire("tensor")
            prog.wait("tensor", eg.wr)
            if nmask:
                prog.wait("tensor", EM[e])
            for jj in range(gn):
                jc = g0 + jj
                lastj = (jc == 31)
                incs = [eg.rd_inc()] if jj == gn - 1 else []
                if lastj:
                    incs.append(pv.wr_inc())
                prog.op("tensor",
                        lambda t, b=pv, eb=eb, jj=jj, jc=jc, h=h,
                        fp=(idx == head_first[h] and jj == 0), lastj=lastj:
                        t.matmul(b.t[0:65, :],
                                 V_sb[:, jc, 65 * h:65 * h + 65],
                                 eb[:, jj * 512:(jj + 1) * 512],
                                 start=fp, stop=lastj),
                        inc=incs or None)
            if idx != head_last[h]:
                continue

            def norm_part1(h=h, ci=ci, b0=b0, pv=pv):
                dg = dng[h % 2]
                dg.acquire("scalar")
                prog.wait("scalar", pv.wr)
                prog.op("scalar", lambda sc, b=pv, h=h: sc.activation(
                    dn[0:1, h % 2, :], b.t[64:65, :], AF.Identity),
                    inc=dg.wr_inc())
                prog.wait("vector", pv.wr)
                prog.op("vector", lambda v, b=pv: v.tensor_copy(
                    A_sb[b0:b0 + 64, ci, :], b.t[0:64, :]),
                    inc=[pv.rd_inc(), dg.wr_inc()])
                prog.wait("vector", dg.wr)
                prog.op("vector", lambda v, h=h: v.reciprocal(
                    dn[0:1, h % 2, :], dn[0:1, h % 2, :]))
                prog.op("vector", lambda v, h=h: v.tensor_copy(
                    dnb[0:1, h % 2, :], dn[0:1, h % 2, :]),
                    inc=dg.wr_inc())

            def norm_part2(h=h, ci=ci, b0=b0, pv=pv):
                dg = dng[h % 2]
                wbk2 = WB[h % 2]
                wbk2.acquire("tensor")
                prog.wait("tensor", dg.wr)
                prog.op("tensor", lambda t, b=wbk2, h=h: t.matmul(
                    b.t[:, :], e0b[:, :], dnb[0:128, h % 2, :], start=True,
                    stop=True), inc=[wbk2.wr_inc(), dg.rd_inc()])
                rbb, rbgb = rb[h % 2], rbg[h % 2]
                rbgb.acquire("vector")
                prog.wait("vector", wbk2.wr)
                prog.op("vector", lambda v, b=wbk2, rbb=rbb: v.tensor_copy(
                    rbb[b0:b0 + 64, :], b.t[b0:b0 + 64, :]),
                    inc=[wbk2.rd_inc(), rbgb.wr_inc()])
                prog.op("vector", lambda v, rbb=rbb: v.tensor_tensor(
                    A_sb[b0:b0 + 64, ci, :], A_sb[b0:b0 + 64, ci, :],
                    rbb[b0:b0 + 64, :], op=ALU.mult),
                    inc=[rbgb.rd_inc(), Ag.wr_inc()])

            norm_part1()
            if pending_norm[0] is not None:
                pending_norm[0]()
            pending_norm[0] = norm_part2
        if pending_norm[0] is not None:
            pending_norm[0]()
            pending_norm[0] = None
        prog.wait("tensor", Ag.wr)
        wqo.rwait("tensor")
        ao.awrite("scalar")
        for oc in range(CK):
            wbk = WB[oc % 2]
            wbk.acquire("tensor")
            for ci2 in range(CK):
                incs = None
                if ci2 == CK - 1:
                    incs = [wbk.wr_inc()]
                    if oc == CK - 1:
                        incs.append(Ag.rd_inc())
                        incs.append(wqo.rd_inc())
                prog.op("tensor", lambda t, b=wbk, oc=oc, ci2=ci2: t.matmul(
                    b.t[:, :], wqo.t[:, ci2, oc * 128:(oc + 1) * 128],
                    A_sb[:, ci2, :], start=(ci2 == 0), stop=(ci2 == CK - 1)),
                    inc=incs)
            prog.wait("scalar", wbk.wr)
            incs = [wbk.rd_inc()]
            if oc == CK - 1:
                incs += [ao.wr_inc(), (negC_rd, 1)]
            prog.op("scalar", lambda sc, b=wbk, oc=oc: sc.activation(
                ao.t[:, oc, :], b.t[:, :], AF.Identity,
                bias=bo_col[:, oc:oc + 1]), inc=incs)
        ao.store(rearr(out_h, t0, t1))
        if s < NS - 1:
            load_wqo(0)
    prog.drain_dmas()
    prog.flush("l0a_main")
    sub2.close()



def phase_L1A(prog, io):
    nc = prog.nc
    x2 = io["x2"]
    wqkv_h, bqkv_h = io["L1_attn_Wqkv"], io["L1_attn_bqkv"]
    wln_h, bln_h = io["L1_attn_Wln"], io["L1_attn_bln"]
    out_h = io["laout"]

    TS = 512
    NS = S // TS
    wqkv = sb(prog, "l1a_wq", [128, CK, 3 * C], BF16)
    wln = sb(prog, "l1a_wl", [128, CK, C], BF16)
    bq_col = sb(prog, "l1a_bq", [128, CK], F32)
    bk_col = sb(prog, "l1a_bk", [128, CK], F32)
    bv_col = sb(prog, "l1a_bvc", [128, CK], F32)
    bln_col = sb(prog, "l1a_bln", [128, CK], F32)
    bv_row = LBuf(prog, "l1a_bvr", [128, C], F32, "gpsimd")
    kv_bf = sb(prog, "l1a_kvb", [128, CK, C], BF16)
    kvbg = Guard(prog, "l1a_kvbg")
    r2 = sb(prog, "l1a_r2", [128, CK], F32)
    scol = sb(prog, "l1a_scol", [128, CK], F32)
    eps24 = sb(prog, "l1a_e24", [128, 1], F32)
    ln64 = sb(prog, "l1a_l64", [128, 1], F32)
    epsr1 = sb(prog, "l1a_er1", [128, 1], F32)
    ones_b = sb(prog, "l1a_one", [128, 128], BF16)
    idb = sb(prog, "l1a_idb", [128, 128], BF16)
    ld = prog.sem("l1a_ld")

    prog.op("gpsimd", lambda g: g.memset(ones_b[:, :], 1.0))
    prog.op("gpsimd", lambda g: g.memset(eps24[:, :], 1e-24))
    prog.op("gpsimd", lambda g: g.memset(ln64[:, :], -float(np.log(64.0))))
    prog.op("gpsimd", lambda g: g.memset(epsr1[:, :], 1e-24))
    prog.op("gpsimd", lambda g: g.memset(r2[:, :], 0.0))
    idf32a = sb(prog, "l1a_idf32", [128, 128], F32)
    make_ident(prog, idf32a)
    prog.flush("l1a_id0")
    prog.op("vector", lambda v: v.tensor_copy(idb[:, :], idf32a[:, :]))
    for ci in range(CK):
        prog.dma("gpsimd", wqkv[:, ci, :],
                 wqkv_h.ap()[ci * 128:(ci + 1) * 128, :], ld)
        prog.dma("gpsimd", wln[:, ci, :],
                 wln_h.ap()[ci * 128:(ci + 1) * 128, :], ld)
    bias_cols_load(prog, bq_col, bqkv_h, CK, 0, ld)
    bias_cols_load(prog, bk_col, bqkv_h, CK, C, ld)
    bias_cols_load(prog, bv_col, bqkv_h, CK, 2 * C, ld)
    bias_cols_load(prog, bln_col, bln_h, CK, 0, ld)
    bv_row.load(bcast_ap(bqkv_h, 2 * C, C))
    if SHARED.get("overlap_prep") and "l1f_csem" in SHARED:
        tp_l1f = [sb(prog, "l1a_tpl1f", [128, 4 * C], BF16)]
        tpg_l1f = Guard(prog, "l1a_tpgl1f")
        tsem_l1f = prog.psem("l1f_tr")
        SHARED["l1f_tsem"] = tsem_l1f
        items_l1f = [(nm, ic) for nm, O, I in L1F_W
                     for ic in range(I // 128)]
        SHARED["_l1f_items"] = items_l1f
    prog.flush("l1a_init")

    sub = ExitStack()
    xs = [LBuf(prog, f"l1a_x{i}", [128, CK, TS], BF16, "gpsimd", sub)
          for i in range(2)]
    kf = sb(prog, "l1a_kf", [128, CK, TS], BF16, sub)
    kfg = Guard(prog, "l1a_kfg")
    k2 = sb(prog, "l1a_k2", [128, CK, TS], BF16, sub)
    k2g = Guard(prog, "l1a_k2g")
    k2a = sb(prog, "l1a_k2a", [128, CK, TS], BF16, sub)
    k2ag = Guard(prog, "l1a_k2ag")
    k2aT = sb(prog, "l1a_k2aT", [128, 4, C], BF16, sub)
    k2aTg = Guard(prog, "l1a_k2aTg")
    vtok = sb(prog, "l1a_vt", [128, 4, C], BF16, sub)
    vtg = Guard(prog, "l1a_vtg")
    kv_acc = sb(prog, "l1a_kv", [128, CK, C], F32, sub)
    stat = sb(prog, "l1a_st", [128, 2, TS], F32, sub)
    stg = Guard(prog, "l1a_stg")
    trash = sb(prog, "l1a_tr", [128, TS], BF16, sub)
    r2t = sb(prog, "l1a_r2t", [128, 1], F32, sub)
    r2tg = Guard(prog, "l1a_r2tg")
    kvg = Guard(prog, "l1a_kvg")
    pb = [Bank(prog, f"l1a_p{i}", stack=sub) for i in range(2)]
    nbk = Bank(prog, "l1a_n", stack=sub)
    tb = [Bank(prog, f"l1a_t{i}", 1024, BF16, stack=sub) for i in range(2)]
    kb = [Bank(prog, f"l1a_kb{i}", stack=sub) for i in range(2)]
    rs = prog.sem("l1a_rs")

    prog.op("gpsimd", lambda g: g.memset(kv_acc[:, :, :], 0.0))
    prog.wait("tensor", ld)
    for s in range(NS):
        bi = s % 2
        t0, t1 = s * TS, (s + 1) * TS
        if SHARED.get("l1f_tsem") is not None and SHARED["_l1f_items"]:
            take, SHARED["_l1f_items"] = (SHARED["_l1f_items"][:7],
                                          SHARED["_l1f_items"][7:])
            emit_l1f_transposes(prog, io, tp_l1f, tpg_l1f, take,
                                SHARED["l1f_csem"], SHARED["l1f_tsem"])
        xs[bi].load(rearr(x2, t0, t1))
        xs[bi].rwait("tensor")
        kfg.acquire("scalar")
        for oc in range(CK):
            bk = pb[oc % 2]
            bk.acquire("tensor")
            for ci in range(CK):
                prog.op("tensor", lambda t, b=bk, oc=oc, ci=ci, bi=bi:
                        t.matmul(b.t[:, :],
                                 wqkv[:, ci, C + oc * 128:C + (oc + 1) * 128],
                                 xs[bi].t[:, ci, :],
                                 start=(ci == 0), stop=(ci == CK - 1)),
                        inc=[bk.wr_inc()] if ci == CK - 1 else None)
            prog.wait("scalar", bk.wr)
            incs = [bk.rd_inc()] + ([kfg.wr_inc()] if oc == CK - 1 else [])
            prog.op("scalar", lambda sc, b=bk, oc=oc: sc.activation(
                kf[:, oc, :], b.t[:, :], AF.Identity,
                bias=bk_col[:, oc:oc + 1]), inc=incs)
        nbk.acquire("tensor")
        prog.wait("tensor", kfg.wr)
        for ci in range(CK):
            prog.op("tensor", lambda t, ci=ci: t.matmul(
                nbk.t[:, :], ones_b[:, :], kf[:, ci, :],
                start=(ci == 0), stop=(ci == CK - 1)),
                inc=[nbk.wr_inc()] if ci == CK - 1 else None)
        stg.acquire("vector")
        prog.wait("vector", nbk.wr)
        prog.op("vector", lambda v: v.tensor_scalar(
            stat[:, 0, :], nbk.t[:, :], 1.0 / C, None, op0=ALU.mult),
            inc=[nbk.rd_inc(), stg.wr_inc()])
        k2g.acquire("scalar")
        prog.wait("vector", stg.wr)
        for ci in range(CK):
            incs = [kfg.rd_inc(), stg.rd_inc()] if ci == CK - 1 else None
            prog.op("vector", lambda v, ci=ci: v.tensor_tensor(
                kf[:, ci, :], kf[:, ci, :], stat[:, 0, :],
                op=ALU.subtract), inc=incs)
        prog.wait("scalar", kfg.rd)
        for ci in range(CK):
            prog.op("scalar", lambda sc, ci=ci: sc.activation(
                k2[:, ci, :], kf[:, ci, :], AF.Square),
                inc=k2g.wr_inc() if ci == CK - 1 else None)
        nbk.acquire("tensor")
        prog.wait("tensor", k2g.wr)
        for ci in range(CK):
            prog.op("tensor", lambda t, ci=ci: t.matmul(
                nbk.t[:, :], ones_b[:, :], k2[:, ci, :],
                start=(ci == 0), stop=(ci == CK - 1)),
                inc=[nbk.wr_inc(), k2g.rd_inc()] if ci == CK - 1 else None)
        stg.acquire("vector")
        prog.wait("vector", nbk.wr)
        prog.op("vector", lambda v: v.tensor_scalar(
            stat[:, 1, :], nbk.t[:, :], 1e-7, None, op0=ALU.add),
            inc=nbk.rd_inc())
        prog.op("vector", lambda v: v.reciprocal(stat[:, 1, :],
                                                 stat[:, 1, :]),
                inc=stg.wr_inc())
        k2ag.acquire("vector")
        prog.wait("vector", stg.wr)
        for ci in range(CK):
            incs = [k2g.rd_inc(), stg.rd_inc(), k2ag.wr_inc()] \
                if ci == CK - 1 else None
            prog.op("vector", lambda v, ci=ci: v.tensor_tensor(
                k2a[:, ci, :], k2[:, ci, :], stat[:, 1, :], op=ALU.mult),
                inc=incs)
        prog.wait("scalar", k2ag.wr)
        for ci in range(CK):
            r2tg.acquire("scalar")
            prog.op("scalar", lambda sc, ci=ci: sc.activation(
                trash[:, :], k2a[:, ci, :], AF.Square,
                accum_out=r2t[:, 0:1]), inc=r2tg.wr_inc())
            prog.wait("vector", r2tg.wr)
            prog.op("vector", lambda v, ci=ci: v.tensor_tensor(
                r2[:, ci:ci + 1], r2[:, ci:ci + 1], r2t[:, 0:1], op=ALU.add),
                inc=r2tg.rd_inc())
        k2aTg.acquire("vector")
        prog.wait("tensor", k2ag.wr)
        for tc4 in range(4):
            for ci in range(CK):
                bk = tb[(tc4 * CK + ci) % 2]
                bk.acquire("tensor")
                incs = [bk.wr_inc()]
                if tc4 == 3 and ci == CK - 1:
                    incs.append(k2ag.rd_inc())
                prog.op("tensor", lambda t, b=bk, tc4=tc4, ci=ci: t.matmul(
                    b.t[:, 0:128], k2a[:, ci, tc4 * 128:(tc4 + 1) * 128],
                    idb[:, :], is_transpose=True), inc=incs)
                prog.wait("vector", bk.wr)
                incs = [bk.rd_inc()]
                if tc4 == 3 and ci == CK - 1:
                    incs.append(k2aTg.wr_inc())
                prog.op("vector", lambda v, b=bk, tc4=tc4, ci=ci:
                        v.tensor_copy(k2aT[:, tc4, ci * 128:(ci + 1) * 128],
                                      b.t[:, 0:128]), inc=incs)
        vtg.acquire("vector")
        bv_row.rwait("vector")
        for tc4 in range(4):
            for ostr in range(2):
                bk = pb[(tc4 * 2 + ostr) % 2]
                bk.acquire("tensor")
                for ci in range(CK):
                    prog.op("tensor",
                            lambda t, b=bk, ci=ci, tc4=tc4, ostr=ostr, bi=bi:
                            t.matmul(b.t[:, :],
                                     xs[bi].t[:, ci,
                                              tc4 * 128:(tc4 + 1) * 128],
                                     wqkv[:, ci,
                                          2 * C + ostr * 512:
                                          2 * C + (ostr + 1) * 512],
                                     start=(ci == 0), stop=(ci == CK - 1)),
                            inc=([bk.wr_inc(), xs[bi].rd_inc()]
                                 if (ci == CK - 1 and tc4 == 3 and ostr == 1)
                                 else ([bk.wr_inc()] if ci == CK - 1
                                       else None)))
                prog.wait("vector", bk.wr)
                incs = [bk.rd_inc()]
                if tc4 == 3 and ostr == 1:
                    incs.append(vtg.wr_inc())
                prog.op("vector", lambda v, b=bk, tc4=tc4, ostr=ostr:
                        v.tensor_tensor(
                            vtok[:, tc4, ostr * 512:(ostr + 1) * 512],
                            b.t[:, :],
                            bv_row.t[:, ostr * 512:(ostr + 1) * 512],
                            op=ALU.add), inc=incs)
        prog.wait("tensor", k2aTg.wr)
        prog.wait("tensor", vtg.wr)
        if s == 0:
            kvg.acquire("vector")
        for ci in range(CK):
            for ds in range(2):
                bk = kb[(ci * 2 + ds) % 2]
                bk.acquire("tensor")
                for tc4 in range(4):
                    incs = None
                    if tc4 == 3:
                        incs = [bk.wr_inc()]
                        if ci == CK - 1 and ds == 1:
                            incs += [k2aTg.rd_inc(), vtg.rd_inc()]
                    prog.op("tensor",
                            lambda t, b=bk, ci=ci, ds=ds, tc4=tc4: t.matmul(
                                b.t[:, :],
                                k2aT[:, tc4, ci * 128:(ci + 1) * 128],
                                vtok[:, tc4, ds * 512:(ds + 1) * 512],
                                start=(tc4 == 0), stop=(tc4 == 3)), inc=incs)
                prog.wait("vector", bk.wr)
                incs = [bk.rd_inc()]
                if s == NS - 1 and ci == CK - 1 and ds == 1:
                    incs.append(kvg.wr_inc())
                prog.op("vector", lambda v, b=bk, ci=ci, ds=ds:
                        v.tensor_tensor(
                            kv_acc[:, ci, ds * 512:(ds + 1) * 512],
                            kv_acc[:, ci, ds * 512:(ds + 1) * 512],
                            b.t[:, :], op=ALU.add), inc=incs)
    kvbg.acquire("vector")
    prog.wait("scalar", kvg.wr)
    prog.op("scalar", lambda sc: sc.activation(
        scol[:, 0:CK], r2[:, 0:CK], AF.Ln, bias=eps24[:, 0:1]))
    prog.op("scalar", lambda sc: sc.activation(
        scol[:, 0:CK], scol[:, 0:CK], AF.Exp, scale=-0.5,
        bias=ln64[:, 0:1]))
    prog.flush("l1a_scol")
    for ci in range(CK):
        incs = [kvbg.wr_inc()] if ci == CK - 1 else None
        prog.op("vector", lambda v, ci=ci: v.tensor_scalar(
            kv_bf[:, ci, :], kv_acc[:, ci, :], scol[:, ci:ci + 1], None,
            op0=ALU.mult), inc=incs)
    prog.drain_dmas()
    prog.flush("l1a_pass1")
    sub.close()

    sub2 = ExitStack()
    xs2 = [LBuf(prog, f"l1a_y{i}", [128, CK, TS], BF16, "gpsimd", sub2)
           for i in range(2)]
    qf = sb(prog, "l1a_qf", [128, CK, TS], BF16, sub2)
    qfg = Guard(prog, "l1a_qfg")
    q2 = sb(prog, "l1a_q2", [128, CK, TS], BF16, sub2)
    q2g = Guard(prog, "l1a_q2g")
    q2n = sb(prog, "l1a_q2n", [128, CK, TS], BF16, sub2)
    q2ng = Guard(prog, "l1a_q2ng")
    u_sb = sb(prog, "l1a_u", [128, CK, TS], BF16, sub2)
    ug = Guard(prog, "l1a_ug")
    stat2 = sb(prog, "l1a_st2", [128, 3, TS], F32, sub2)
    stg2 = Guard(prog, "l1a_stg2")
    ob = [OBuf(prog, f"l1a_o{i}", [128, CK, TS], F32, stack=sub2)
          for i in range(2)]
    pb2 = [Bank(prog, f"l1a_q{i}", stack=sub2) for i in range(2)]
    nb2 = Bank(prog, "l1a_n2", stack=sub2)
    rs2 = prog.sem("l1a_rs2")

    prog.wait("tensor", kvbg.wr)
    for s in range(NS):
        bi = s % 2
        t0, t1 = s * TS, (s + 1) * TS
        if SHARED.get("l1f_tsem") is not None and SHARED["_l1f_items"]:
            take, SHARED["_l1f_items"] = (SHARED["_l1f_items"][:7],
                                          SHARED["_l1f_items"][7:])
            emit_l1f_transposes(prog, io, tp_l1f, tpg_l1f, take,
                                SHARED["l1f_csem"], SHARED["l1f_tsem"])
        xs2[bi].load(rearr(x2, t0, t1))
        xs2[bi].rwait("tensor")
        qfg.acquire("scalar")
        for oc in range(CK):
            bk = pb2[oc % 2]
            bk.acquire("tensor")
            for ci in range(CK):
                prog.op("tensor", lambda t, b=bk, oc=oc, ci=ci, bi=bi:
                        t.matmul(b.t[:, :],
                                 wqkv[:, ci, oc * 128:(oc + 1) * 128],
                                 xs2[bi].t[:, ci, :],
                                 start=(ci == 0), stop=(ci == CK - 1)),
                        inc=[bk.wr_inc()] if ci == CK - 1 else None)
            prog.wait("scalar", bk.wr)
            incs = [bk.rd_inc()] + ([qfg.wr_inc()] if oc == CK - 1 else [])
            prog.op("scalar", lambda sc, b=bk, oc=oc: sc.activation(
                qf[:, oc, :], b.t[:, :], AF.Identity,
                bias=bq_col[:, oc:oc + 1]), inc=incs)
        nb2.acquire("tensor")
        prog.wait("tensor", qfg.wr)
        for ci in range(CK):
            prog.op("tensor", lambda t, ci=ci: t.matmul(
                nb2.t[:, :], ones_b[:, :], qf[:, ci, :],
                start=(ci == 0), stop=(ci == CK - 1)),
                inc=[nb2.wr_inc()] if ci == CK - 1 else None)
        stg2.acquire("vector")
        prog.wait("vector", nb2.wr)
        prog.op("vector", lambda v: v.tensor_scalar(
            stat2[:, 0, :], nb2.t[:, :], 1.0 / C, None, op0=ALU.mult),
            inc=[nb2.rd_inc(), stg2.wr_inc()])
        q2g.acquire("scalar")
        prog.wait("vector", stg2.wr)
        for ci in range(CK):
            incs = [qfg.rd_inc(), stg2.rd_inc()] if ci == CK - 1 else None
            prog.op("vector", lambda v, ci=ci: v.tensor_tensor(
                qf[:, ci, :], qf[:, ci, :], stat2[:, 0, :],
                op=ALU.subtract), inc=incs)
        prog.wait("scalar", qfg.rd)
        for ci in range(CK):
            prog.op("scalar", lambda sc, ci=ci: sc.activation(
                q2[:, ci, :], qf[:, ci, :], AF.Square),
                inc=q2g.wr_inc() if ci == CK - 1 else None)
        nb2.acquire("tensor")
        prog.wait("tensor", q2g.wr)
        for ci in range(CK):
            prog.op("tensor", lambda t, ci=ci: t.matmul(
                nb2.t[:, :], ones_b[:, :], q2[:, ci, :],
                start=(ci == 0), stop=(ci == CK - 1)),
                inc=[nb2.wr_inc(), q2g.rd_inc()] if ci == CK - 1 else None)
        stg2.acquire("vector")
        prog.wait("vector", nb2.wr)
        prog.op("vector", lambda v: v.tensor_scalar(
            stat2[:, 1, :], nb2.t[:, :], 1e-7, None, op0=ALU.add),
            inc=nb2.rd_inc())
        prog.op("vector", lambda v: v.reciprocal(stat2[:, 1, :],
                                                 stat2[:, 1, :]),
                inc=stg2.wr_inc())
        q2ng.acquire("vector")
        q2aw = prog.sem() if s == 0 else q2aw
        prog.wait("vector", stg2.wr)
        for ci in range(CK):
            incs = ([stg2.rd_inc(), (q2aw, 1), q2g.rd_inc()]
                    if ci == CK - 1 else None)
            prog.op("vector", lambda v, ci=ci: v.tensor_tensor(
                q2n[:, ci, :], q2[:, ci, :], stat2[:, 1, :], op=ALU.mult),
                inc=incs)
        prog.wait("scalar", q2aw)
        sq2d = prog.sem() if s == 0 else sq2d
        for ci in range(CK):
            incs = [(sq2d, 1)] if ci == CK - 1 else None
            prog.op("scalar", lambda sc, ci=ci: sc.activation(
                q2[:, ci, :], q2n[:, ci, :], AF.Square), inc=incs)
        nb2.acquire("tensor")
        prog.wait("tensor", sq2d)
        for ci in range(CK):
            prog.op("tensor", lambda t, ci=ci: t.matmul(
                nb2.t[:, :], ones_b[:, :], q2[:, ci, :],
                start=(ci == 0), stop=(ci == CK - 1)),
                inc=[nb2.wr_inc(), q2g.rd_inc()] if ci == CK - 1 else None)
        stg2.acquire("scalar")
        prog.wait("scalar", nb2.wr)
        prog.op("scalar", lambda sc: sc.activation(
            stat2[:, 2, :], nb2.t[:, :], AF.Ln, bias=epsr1[:, 0:1]),
            inc=nb2.rd_inc())
        prog.op("scalar", lambda sc: sc.activation(
            stat2[:, 2, :], stat2[:, 2, :], AF.Exp, scale=-0.5),
            inc=stg2.wr_inc())
        prog.wait("vector", stg2.wr)
        for ci in range(CK):
            incs = [stg2.rd_inc(), q2ng.wr_inc()] if ci == CK - 1 else None
            prog.op("vector", lambda v, ci=ci: v.tensor_tensor(
                q2n[:, ci, :], q2n[:, ci, :], stat2[:, 2, :], op=ALU.mult),
                inc=incs)
        ug.acquire("scalar")
        prog.wait("tensor", q2ng.wr)
        for dc in range(CK):
            bk = pb2[dc % 2]
            bk.acquire("tensor")
            for ci in range(CK):
                prog.op("tensor", lambda t, b=bk, dc=dc, ci=ci, bi=bi:
                        t.matmul(b.t[:, :],
                                 wqkv[:, ci,
                                      2 * C + dc * 128:2 * C + (dc + 1) * 128],
                                 xs2[bi].t[:, ci, :],
                                 start=(ci == 0), stop=False), inc=None)
            for ci in range(CK):
                incs = None
                if ci == CK - 1:
                    incs = [bk.wr_inc()]
                    if dc == CK - 1:
                        incs += [q2ng.rd_inc(), xs2[bi].rd_inc()]
                prog.op("tensor", lambda t, b=bk, dc=dc, ci=ci: t.matmul(
                    b.t[:, :], kv_bf[:, ci, dc * 128:(dc + 1) * 128],
                    q2n[:, ci, :], start=False, stop=(ci == CK - 1)),
                    inc=incs)
            prog.wait("scalar", bk.wr)
            incs = [bk.rd_inc()] + ([ug.wr_inc()] if dc == CK - 1 else [])
            prog.op("scalar", lambda sc, b=bk, dc=dc: sc.activation(
                u_sb[:, dc, :], b.t[:, :], AF.Identity,
                bias=bv_col[:, dc:dc + 1]), inc=incs)
        prog.wait("tensor", ug.wr)
        ob[bi].awrite("scalar")
        for oc in range(CK):
            bk = pb2[oc % 2]
            bk.acquire("tensor")
            for ci in range(CK):
                incs = None
                if ci == CK - 1:
                    incs = [bk.wr_inc()]
                    if oc == CK - 1:
                        incs.append(ug.rd_inc())
                prog.op("tensor", lambda t, b=bk, oc=oc, ci=ci: t.matmul(
                    b.t[:, :], wln[:, ci, oc * 128:(oc + 1) * 128],
                    u_sb[:, ci, :], start=(ci == 0), stop=(ci == CK - 1)),
                    inc=incs)
            prog.wait("scalar", bk.wr)
            incs = [bk.rd_inc()] + ([ob[bi].wr_inc()] if oc == CK - 1
                                    else [])
            prog.op("scalar", lambda sc, b=bk, oc=oc, bi=bi: sc.activation(
                ob[bi].t[:, oc, :], b.t[:, :], AF.Identity,
                bias=bln_col[:, oc:oc + 1]), inc=incs)
        ob[bi].store(rearr(out_h, t0, t1))
    prog.drain_dmas()
    prog.flush("l1a_pass2")
    sub2.close()



L1F_W = [("w1", 4 * C, C), ("w2", 4 * C, C), ("h1", 4 * C, 4 * C),
         ("h2", 4 * C, 4 * C), ("w3", C, 4 * C)]


def l1f_scratch(nc):
    if "l1f_wc" not in SHARED:
        wc, wt = {}, {}
        for nm, O, I in L1F_W:
            wc[nm] = nc.dram_tensor(f"l1f_{nm}c", [O, I], BF16,
                                    kind="Internal")
            wt[nm] = nc.dram_tensor(f"l1f_{nm}t", [I, O], BF16,
                                    kind="Internal")
        SHARED["l1f_wc"], SHARED["l1f_wt"] = wc, wt
    return SHARED["l1f_wc"], SHARED["l1f_wt"]


def l1f_cast_chunks(io):
    chunks = []
    for nm, O, I in L1F_W:
        nsp = 2 if O * I > 8 * 1024 * 1024 else 1
        step = O // nsp
        for i in range(nsp):
            chunks.append((nm, i * step, (i + 1) * step))
    return chunks


def emit_l1f_cast(prog, io, idx, sem):
    wc, _ = l1f_scratch(prog.nc)
    chunks = l1f_cast_chunks(io)
    if idx >= len(chunks):
        return 0
    nm, r0, r1 = chunks[idx]
    prog.op("gpsimd", lambda g, nm=nm, r0=r0, r1=r1: g.dma_start(
        wc[nm].ap()[r0:r1, :], io[f"L1_ffn_{nm}_w"].ap()[r0:r1, :],
        max_dma_last_dim=8192), inc=(sem, 16))
    return 1


def emit_l1f_transposes(prog, io, tp, tpg, items, csem, tsem):
    _, wt = l1f_scratch(prog.nc)
    wc = SHARED["l1f_wc"]
    sizes = dict((nm, (O, I)) for nm, O, I in L1F_W)
    for k, (nm, ic) in enumerate(items):
        O, I = sizes[nm]
        buf = tp[0]
        tpg.acquire("sync")
        prog.wait("sync", csem)
        prog.op("sync", lambda e, nm=nm, ic=ic, O=O, buf=buf: e.dma_start(
            buf[:, 0:O], wc[nm].ap()[0:O, ic * 128:(ic + 1) * 128],
            transpose=True), inc=(tpg.wr, 16))
        prog.wait("sync", tpg.wr)
        prog.op("sync", lambda e, nm=nm, ic=ic, O=O, buf=buf: e.dma_start(
            wt[nm].ap()[ic * 128:(ic + 1) * 128, 0:O], buf[:, 0:O]),
            inc=(tsem, 16))
        prog.wait("sync", tsem)
        prog.sig("sync", tpg.rd_inc())


def phase_L1F(prog, io):
    nc = prog.nc
    x3, out_h = io["x3"], io["f2out"]
    TS = 512
    NS = S // TS

    wc, wt = l1f_scratch(nc)
    prep_done = "l1f_tsem" in SHARED
    if not prep_done:
        csem = prog.sem("l1f_cast")
        for nm, O, I in L1F_W:
            prog.dma("gpsimd", wc[nm].ap(), io[f"L1_ffn_{nm}_w"].ap(), csem,
                     max_dma_last_dim=8192)
    s1c = sb(prog, "l1f_s1", [128, HK], F32)
    s2c = sb(prog, "l1f_s2", [128, HK], F32)
    h1c = sb(prog, "l1f_h1", [128, HK], F32)
    h2c = sb(prog, "l1f_h2", [128, HK], F32)
    s3c = sb(prog, "l1f_s3", [128, CK], F32)
    bias_cols_load(prog, s1c, io["L1_ffn_w1_s"], HK)
    bias_cols_load(prog, s2c, io["L1_ffn_w2_s"], HK)
    bias_cols_load(prog, h1c, io["L1_ffn_h1_s"], HK)
    bias_cols_load(prog, h2c, io["L1_ffn_h2_s"], HK)
    bias_cols_load(prog, s3c, io["L1_ffn_w3_s"], CK)
    prog.flush("l1f_init")

    if not prep_done:
        tp = [LBuf(prog, f"l1f_tp{i}", [128, 4 * C], BF16, "sync")
              for i in range(2)]
        tsem = prog.sem("l1f_tst")
        prog.wait("sync", csem)
        ti = 0
        for nm, O, I in L1F_W:
            for ic in range(I // 128):
                buf = tp[ti % 2]
                ti += 1
                buf.load(wc[nm].ap()[0:O, ic * 128:(ic + 1) * 128],
                         dst=buf.t[:, 0:O], transpose=True)
                buf.rwait("sync")
                prog.dma("sync", wt[nm].ap()[ic * 128:(ic + 1) * 128, 0:O],
                         buf.t[:, 0:O], tsem)
                prog.wait("sync", tsem)
                prog.sig("sync", buf.rd_inc())
        prog.wait("sync", tsem)
    else:
        ts_sh = SHARED["l1f_tsem"]
        prog.wait("sync", ts_sh)
        prog.wait("gpsimd", ts_sh)
    prog.flush("l1f_transp")

    if globals().get("DBG_L1F") == 1:
        dsem = prog.sem()
        prog.dma("gpsimd", out_h.ap(), wt["w1"].ap(), dsem)
        prog.drain_dmas()
        prog.flush("l1f_dbg")
        return

    xs = [LBuf(prog, f"l1f_x{i}", [128, CK, TS], BF16, "gpsimd")
          for i in range(2)]
    g_sb = sb(prog, "l1f_g", [128, HK, TS], BF16)
    gg = Guard(prog, "l1f_gg")
    u_sb = sb(prog, "l1f_u", [128, HK, TS], BF16)
    ug = Guard(prog, "l1f_ug")
    silb = sb(prog, "l1f_sil", [128, TS], BF16)
    silg = Guard(prog, "l1f_silg")
    y2b = sb(prog, "l1f_y2b", [128, TS], BF16)
    y2g = Guard(prog, "l1f_y2g")
    wg12 = [LBuf(prog, "l1f_w12g0", [128, CK, 2048], BF16, "sync")]
    hg = [LBuf(prog, f"l1f_hg{i}", [128, HK, 512], BF16, "sync")
          for i in range(2)]
    ob = [OBuf(prog, f"l1f_o{i}", [128, TS], F32) for i in range(2)]
    yb = [Bank(prog, f"l1f_yb{i}") for i in range(4)]
    hb = [Bank(prog, f"l1f_hb{i}") for i in range(2)]
    w3b = [Bank(prog, f"l1f_w3b{i}") for i in range(2)]

    def slab12(og):
        buf = wg12[0]
        ap1 = wt["w1"].ap().rearrange("(a p) o -> p a o", p=128)
        ap2 = wt["w2"].ap().rearrange("(a p) o -> p a o", p=128)
        if buf.rounds > 0:
            prog.wait(buf.eng, buf.rd)
        prog.dma(buf.eng, buf.t[:, :, 0:1024],
                 ap1[:, :, og * 1024:(og + 1) * 1024], buf.ld)
        prog.dma(buf.eng, buf.t[:, :, 1024:2048],
                 ap2[:, :, og * 1024:(og + 1) * 1024], buf.ld)
        buf.ldv = buf.ld.n
        buf.rounds += 1
        return buf

    def slabh(nm, og):
        buf = hg[og % 2]
        ap = wt[nm].ap().rearrange("(a p) o -> p a o", p=128)
        buf.load(ap[:, :, og * 512:(og + 1) * 512])
        return buf

    for s in range(NS):
        bi = s % 2
        t0, t1 = s * TS, (s + 1) * TS
        xs[bi].load(rearr(x3, t0, t1))
        xs[bi].rwait("tensor")
        gg.acquire("vector")
        for og in range(4):
            buf = slab12(og)
            buf.rwait("tensor")
            for oj in range(8):
                oc = og * 8 + oj
                b1 = yb[2 * (oj % 2)]
                b2 = yb[2 * (oj % 2) + 1]
                b1.acquire("tensor")
                b2.acquire("tensor")
                for ci in range(CK):
                    prog.op("tensor", lambda t, b=b1, buf=buf, oj=oj, ci=ci,
                            bi=bi: t.matmul(
                                b.t[:, :],
                                buf.t[:, ci, oj * 128:(oj + 1) * 128],
                                xs[bi].t[:, ci, :],
                                start=(ci == 0), stop=(ci == CK - 1)),
                            inc=[b1.wr_inc()] if ci == CK - 1 else None)
                for ci in range(CK):
                    incs = None
                    if ci == CK - 1:
                        incs = [b2.wr_inc()]
                        if oj == 7:
                            incs.append((buf.rd, 1))
                    prog.op("tensor", lambda t, b=b2, buf=buf, oj=oj, ci=ci,
                            bi=bi: t.matmul(
                                b.t[:, :],
                                buf.t[:, ci, 1024 + oj * 128:
                                      1024 + (oj + 1) * 128],
                                xs[bi].t[:, ci, :],
                                start=(ci == 0), stop=(ci == CK - 1)),
                            inc=incs)
                silg.acquire("scalar")
                prog.wait("scalar", b1.wr)
                prog.op("scalar", lambda sc, b=b1, oc=oc: sc.activation(
                    silb[:, :], b.t[:, :], AF.Silu,
                    scale=s1c[:, oc:oc + 1]),
                    inc=[b1.rd_inc(), silg.wr_inc()])
                y2g.acquire("vector")
                prog.wait("vector", b2.wr)
                prog.op("vector", lambda v, b=b2, oc=oc: v.tensor_scalar(
                    y2b[:, :], b.t[:, :], s2c[:, oc:oc + 1], None,
                    op0=ALU.mult), inc=[b2.rd_inc(), y2g.wr_inc()])
                prog.wait("vector", silg.wr)
                prog.op("vector", lambda v, oc=oc: v.tensor_tensor(
                    g_sb[:, oc, :], silb[:, :], y2b[:, :], op=ALU.mult),
                    inc=[silg.rd_inc(), y2g.rd_inc(), gg.wr_inc()])
        if globals().get("DBG_L1F") == 2 and s == 0:
            dsem = prog.sem("dbg2")
            prog.wait("gpsimd", gg.wr)
            for ic in range(CK):
                prog.dma("gpsimd", rearr(out_h, 0, TS)[:, ic, :],
                         g_sb[:, ic, :], dsem)
            prog.drain_dmas()
            prog.flush("l1f_dbg2")
            return
        ug.acquire("scalar")
        prog.wait("tensor", gg.wr)
        hbufs = [slabh("h1", 0)]
        for og in range(8):
            if og < 7:
                hbufs.append(slabh("h1", og + 1))
            buf = hbufs[og]
            buf.rwait("tensor")
            for oj in range(4):
                oc = og * 4 + oj
                bk = hb[oj % 2]
                bk.acquire("tensor")
                for ic in range(HK):
                    incs = None
                    if ic == HK - 1:
                        incs = [bk.wr_inc()]
                        if oj == 3:
                            incs.append((buf.rd, 1))
                        if oc == HK - 1:
                            incs.append(gg.rd_inc())
                    prog.op("tensor", lambda t, b=bk, buf=buf, oj=oj, ic=ic:
                            t.matmul(b.t[:, :],
                                     buf.t[:, ic, oj * 128:(oj + 1) * 128],
                                     g_sb[:, ic, :],
                                     start=(ic == 0), stop=(ic == HK - 1)),
                            inc=incs)
                prog.wait("scalar", bk.wr)
                incs = [bk.rd_inc()] + ([ug.wr_inc()])
                prog.op("scalar", lambda sc, b=bk, oc=oc: sc.activation(
                    u_sb[:, oc, :], b.t[:, :], AF.Gelu,
                    scale=h1c[:, oc:oc + 1]), inc=incs)
        gg.acquire("scalar")
        prog.wait("tensor", ug.wr)
        hbufs = [slabh("h2", 0)]
        for og in range(8):
            if og < 7:
                hbufs.append(slabh("h2", og + 1))
            buf = hbufs[og]
            buf.rwait("tensor")
            for oj in range(4):
                oc = og * 4 + oj
                bk = hb[oj % 2]
                bk.acquire("tensor")
                for ic in range(HK):
                    incs = None
                    if ic == HK - 1:
                        incs = [bk.wr_inc()]
                        if oj == 3:
                            incs.append((buf.rd, 1))
                        if oc == HK - 1:
                            incs.append(ug.rd_inc())
                    prog.op("tensor", lambda t, b=bk, buf=buf, oj=oj, ic=ic:
                            t.matmul(b.t[:, :],
                                     buf.t[:, ic, oj * 128:(oj + 1) * 128],
                                     u_sb[:, ic, :],
                                     start=(ic == 0), stop=(ic == HK - 1)),
                            inc=incs)
                prog.wait("scalar", bk.wr)
                incs = [bk.rd_inc()] + ([gg.wr_inc()])
                prog.op("scalar", lambda sc, b=bk, oc=oc: sc.activation(
                    g_sb[:, oc, :], b.t[:, :], AF.Gelu,
                    scale=h2c[:, oc:oc + 1]), inc=incs)
        prog.wait("tensor", gg.wr)
        hbufs = [slabh("w3", 0)]
        for og in range(2):
            if og < 1:
                hbufs.append(slabh("w3", og + 1))
            buf = hbufs[og]
            buf.rwait("tensor")
            for oj in range(4):
                oc = og * 4 + oj
                bk = w3b[oj % 2]
                bk.acquire("tensor")
                for ic in range(HK):
                    incs = None
                    if ic == HK - 1:
                        incs = [bk.wr_inc()]
                        if oj == 3:
                            incs.append((buf.rd, 1))
                        if oc == CK - 1:
                            incs.append(gg.rd_inc())
                            incs.append(xs[bi].rd_inc())
                    prog.op("tensor", lambda t, b=bk, buf=buf, oj=oj, ic=ic:
                            t.matmul(b.t[:, :],
                                     buf.t[:, ic, oj * 128:(oj + 1) * 128],
                                     g_sb[:, ic, :],
                                     start=(ic == 0), stop=(ic == HK - 1)),
                            inc=incs)
                obb = ob[oc % 2]
                obb.awrite("vector")
                prog.wait("vector", bk.wr)
                prog.op("vector", lambda v, b=bk, oc=oc, obb=obb:
                        v.tensor_scalar(obb.t[:, :], b.t[:, :],
                                        s3c[:, oc:oc + 1], None,
                                        op0=ALU.mult),
                        inc=[bk.rd_inc(), obb.wr_inc()])
                obb.store(rearr(out_h, t0, t1)[:, oc, :])



PHASE_IO = {
    "T": (["x_in"], ["xfm"]),
    "L0A": (["xfm", "L0_attn_w_qkv", "L0_attn_b_qkv", "L0_attn_w_out",
             "L0_attn_b_out"], ["attnout"]),
    "LN1": (["xfm", "attnout"], ["x1"]),
    "L0F": (["x1", "L0_ffn_weight", "L0_ffn_alpha", "L0_ffn_bias"],
            ["ffnout"]),
    "LN2": (["x1", "ffnout"], ["x2"]),
    "L1A": (["x2", "L1_attn_Wqkv", "L1_attn_bqkv", "L1_attn_Wln",
             "L1_attn_bln"], ["laout"]),
    "LN3": (["x2", "laout"], ["x3"]),
    "L1F": (["x3", "L1_ffn_w1_w", "L1_ffn_w1_s", "L1_ffn_w2_w",
             "L1_ffn_w2_s", "L1_ffn_h1_w", "L1_ffn_h1_s", "L1_ffn_h2_w",
             "L1_ffn_h2_s", "L1_ffn_w3_w", "L1_ffn_w3_s"], ["f2out"]),
    "LN4": (["x3", "f2out"], ["x4"]),
    "OUTT": (["x4"], ["out"]),
}

DRAM_SPECS = {
    "x_in": ([S, C], F32), "xfm": ([C, S], F32), "attnout": ([C, S], F32),
    "x1": ([C, S], F32), "ffnout": ([C, S], F32), "x2": ([C, S], F32),
    "laout": ([C, S], F32), "x3": ([C, S], F32), "f2out": ([C, S], F32),
    "x4": ([C, S], F32), "out": ([S, C], F32),
    "L0_attn_w_qkv": ([C, 3 * C], F32), "L0_attn_b_qkv": ([3 * C], F32),
    "L0_attn_w_out": ([C, C], F32), "L0_attn_b_out": ([C], F32),
    "L0_ffn_weight": ([C, C], F32), "L0_ffn_alpha": ([1], F32),
    "L0_ffn_bias": ([C], F32),
    "L1_attn_Wqkv": ([C, 3 * C], F32), "L1_attn_bqkv": ([3 * C], F32),
    "L1_attn_Wln": ([C, C], F32), "L1_attn_bln": ([C], F32),
    "L1_ffn_w1_w": ([4 * C, C], F32), "L1_ffn_w1_s": ([4 * C], F32),
    "L1_ffn_w2_w": ([4 * C, C], F32), "L1_ffn_w2_s": ([4 * C], F32),
    "L1_ffn_h1_w": ([4 * C, 4 * C], F32), "L1_ffn_h1_s": ([4 * C], F32),
    "L1_ffn_h2_w": ([4 * C, 4 * C], F32), "L1_ffn_h2_s": ([4 * C], F32),
    "L1_ffn_w3_w": ([C, 4 * C], F32), "L1_ffn_w3_s": ([C], F32),
}

PHASE_FN = {
    "T": phase_T,
    "OUTT": phase_OUTT,
    "L0A": phase_L0A,
    "L0F": phase_L0F,
    "L1A": phase_L1A,
    "L1F": phase_L1F,
    "LN1": lambda p, io: phase_LN(p, io, "xfm", "attnout", "x1", "ln1"),
    "LN2": lambda p, io: phase_LN(p, io, "x1", "ffnout", "x2", "ln2"),
    "LN3": lambda p, io: phase_LN(p, io, "x2", "laout", "x3", "ln3"),
    "LN4": lambda p, io: phase_LN(p, io, "x3", "f2out", "x4", "ln4"),
}


def build_group(phases, needed_later):
    SHARED.clear()
    nc = bass.Bass("TRN2", target_bir_lowering=False)
    if "L1F" in phases and "L0A" in phases:
        SHARED["overlap_prep"] = True
    reads, writes = set(), set()
    for ph in phases:
        r, w = PHASE_IO[ph]
        for t in r:
            if t not in writes:
                reads.add(t)
        writes.update(w)
    ext_in = sorted(reads)
    ext_out = sorted(w for w in writes if w in needed_later)
    handles = {}
    for t in ext_in:
        shp, dt = DRAM_SPECS[t]
        handles[t] = nc.dram_tensor(t, shp, dt, kind="ExternalInput")
    for t in sorted(writes):
        shp, dt = DRAM_SPECS[t]
        kind = "ExternalOutput" if t in ext_out else "Internal"
        handles[t] = nc.dram_tensor(t, shp, dt, kind=kind)
    for ph in phases:
        with ExitStack() as stack:
            prog = Prog(nc, stack)
            PHASE_FN[ph](prog, handles)
            prog.end_phase(ph)
    return nc, ext_in, ext_out


GROUPS = [["T", "L0A", "LN1", "L0F", "LN2", "L1A", "LN3", "L1F", "LN4",
           "OUTT"]]


def kernel(**inputs):
    vals = [dict() for _ in range(NCORE)]
    x = np.ascontiguousarray(np.asarray(inputs["L0_attn_x"],
                                        dtype=np.float32))
    for c in range(NCORE):
        vals[c]["x_in"] = x[c]
        for k, v in inputs.items():
            if k != "L0_attn_x":
                vals[c][k] = np.ascontiguousarray(np.asarray(v))
    for gi, phases in enumerate(GROUPS):
        future_reads = set()
        for phs in GROUPS[gi + 1:]:
            for ph in phs:
                future_reads |= set(PHASE_IO[ph][0])
        needed = future_reads | {"out"}
        nc, ins, outs = build_group(phases, needed)
        in_maps = [{t: vals[c][t] for t in ins} for c in range(NCORE)]
        res = run_bass_kernel_spmd(nc, in_maps, core_ids=list(range(NCORE)))
        for c in range(NCORE):
            for t in outs:
                vals[c][t] = np.asarray(res.results[c][t])
    return np.stack([vals[c]["out"] for c in range(NCORE)], axis=0)
